# version 11
# baseline (speedup 1.0000x reference)
import numpy as np
import ml_dtypes
import concourse.bass as bass
import concourse.mybir as mybir
from concourse.bass_utils import run_bass_kernel_spmd

F32 = mybir.dt.float32
BF16 = mybir.dt.float16
F16 = mybir.dt.float16
U8 = mybir.dt.uint8
AF = mybir.ActivationFunctionType
ALU = mybir.AluOpType
AXX = mybir.AxisListType.X

D = 2048; T = 2048; H = 16; II = 8192
KC = D // 128
NS = II * D
NB = NS // 128
NCTRL = 130156
NT = NB // 128
NSUP = NT // 8
LN_EPS = 1e-5

_HOST = {}

def _host_tables():
    if _HOST:
        return _HOST
    t = np.linspace(0.0, NCTRL - 1.0, NS, dtype=np.float64)
    ii = np.clip(np.floor(t).astype(np.int64), 0, NCTRL - 2)
    f = (t - ii).astype(np.float32)
    f2 = f * f
    f3 = f2 * f
    iblk = ii.reshape(NB, 128)
    I0 = iblk[:, 0].copy()
    assert int((iblk - I0[:, None]).max()) <= 1
    mskB = (iblk != I0[:, None]).astype(np.uint8).reshape(-1)
    tap_idx = np.clip(I0[:, None] + np.arange(-1, 4)[None, :], 0, NCTRL - 1)
    p = np.arange(128)[:, None]; c = np.arange(KC)[None, :]
    _HOST.update(
        f1=f, f2=f2, f3=f3, msk=mskB, tap_idx=tap_idx,
        jrev=np.ascontiguousarray(((T - 1) - (128 * c + p)).astype(np.float32)),
        eye=np.eye(128, dtype=np.float16),
        ones128=np.ones((128, 128), dtype=np.float16),
        onescol=np.ones((1, 128), dtype=np.float32),
    )
    return _HOST

def _t5_planes(cp):
    tabs = _host_tables()
    t5 = np.asarray(cp, dtype=np.float32)[tabs["tap_idx"]]
    return np.ascontiguousarray(t5.reshape(NT, 128, 5).transpose(2, 1, 0))

class _Sem:
    __slots__ = ("h", "name", "cnt")
    def __init__(self, h, name):
        self.h = h; self.name = name; self.cnt = 0

class _B:
    ENGS = ("sync", "gpsimd", "vector", "scalar", "tensor")
    def __init__(self, sems, dmasems):
        self.streams = {e: [] for e in self.ENGS}
        self.sems = sems
        self.dmasems = dmasems
        self.waited = {}
        self.lastw = {}
        self.readers = {}
    def _target(self, semname, cnt):
        if semname in self.dmasems:
            return self.sems[semname].cnt
        return cnt
    def _wait(self, eng, semname, cnt):
        if cnt <= 0:
            return
        key = (eng, semname)
        if self.waited.get(key, 0) >= cnt:
            return
        self.waited[key] = cnt
        h = self.sems[semname].h
        self.streams[eng].append(lambda e, h=h, c=cnt: e.wait_ge(h, c))
    def rd(self, eng, *bufs):
        for b in bufs:
            w = self.lastw.get(b)
            if w:
                self._wait(eng, w[0], self._target(w[0], w[1]))
    def war(self, eng, *bufs):
        for b in bufs:
            w = self.lastw.get(b)
            if w:
                self._wait(eng, w[0], self._target(w[0], w[1]))
            for sn, c in self.readers.get(b, {}).items():
                self._wait(eng, sn, self._target(sn, c))
    def op(self, eng, fn, semname, inc=1, w=(), r=()):
        self.rd(eng, *r)
        self.war(eng, *w)
        s = self.sems[semname]
        s.cnt += inc
        cnt = s.cnt
        h = s.h
        self.streams[eng].append(lambda e, fn=fn, h=h, i=inc: fn(e).then_inc(h, i))
        for b in w:
            self.lastw[b] = (semname, cnt)
            self.readers[b] = {}
        for b in r:
            self.readers.setdefault(b, {})[semname] = cnt
        return cnt
    def plain(self, eng, fn):
        self.streams[eng].append(fn)

_NC_CACHE = {}

def build_nc(debug=()):
    key = ("v1", tuple(sorted(debug)))
    if key in _NC_CACHE:
        return _NC_CACHE[key]
    nc = bass.Bass("TRN2", target_bir_lowering=False)

    def inp(name, shape, dt=F32):
        return nc.dram_tensor(name, list(shape), dt, kind="ExternalInput").ap()

    xT = inp("xT", (D, T))
    xp = inp("xp", (T, D))
    wr_ = [inp("wq", (D, D)), inp("wv", (D, D)), inp("wg", (D, D)), inp("wo", (D, D))]
    beta = inp("beta", (1, H))
    lnww = inp("lnw", (128, KC)); lnbb = inp("lnb", (128, KC))
    jrev = inp("jrev", (128, KC))
    eye = inp("eye", (128, 128), BF16)
    ones128 = inp("ones128", (128, 128), BF16)
    onescol = inp("onescol", (1, 128))
    f1t = inp("f1t", (NS,)); f2t = inp("f2t", (NS,)); f3t = inp("f3t", (NS,))
    mskt = inp("mskt", (NS,), U8)
    t5in = [inp("t5g", (5, 128, NT)), inp("t5u", (5, 128, NT)), inp("t5d", (5, 128, NT))]

    oT = nc.dram_tensor("oT", [D, T], F32, kind="ExternalOutput").ap()
    dbg = {}
    _dbgdt = {"d_x1": BF16, "d_y": BF16, "d_xmid": F32, "d_x2": BF16,
              "d_g": BF16, "d_state": BF16, "d_carry": F32, "d_gw": BF16,
              "d_dw": BF16, "d_inter": BF16}
    for nm, shape in (("d_x1", (D, T)), ("d_y", (D, T)), ("d_xmid", (D, T)),
                      ("d_x2", (D, T)), ("d_g", (D, T)), ("d_state", (D, T)),
                      ("d_carry", (128, H)), ("d_gw", (II, D)), ("d_dw", (D, II)),
                      ("d_inter", (II, T))):
        if nm in debug:
            dbg[nm] = nc.dram_tensor(nm, list(shape), _dbgdt[nm],
                                     kind="ExternalOutput").ap()

    def dram(name, shape, dt):
        return nc.dram_tensor(name, list(shape), dt, kind="Internal").ap()

    w_b = [dram("wq_b", (D, D), BF16), dram("wv_b", (D, D), BF16),
           dram("wg_b", (D, D), BF16), dram("wo_b", (D, D), BF16)]
    w_bT = [dram("wq_bT", (D, D), BF16), dram("wv_bT", (D, D), BF16),
            dram("wg_bT", (D, D), BF16), dram("wo_bT", (D, D), BF16)]
    g_dn = dram("g_dn", (D, II), BF16)
    g_gate = dram("g_gate", (II, D), BF16)
    g_up = dram("g_up", (II, D), BF16)
    gT_dn = dram("g_dnT", (II, D), BF16)
    gT_gate = dram("g_gateT", (D, II), BF16)
    gT_up = dram("g_upT", (D, II), BF16)
    y_dr = dram("y_dr", (D, T), BF16)
    gg_dr = dram("gg_dr", (D, T), BF16)
    xmid_dr = dram("xmid_dr", (D, T), F32)
    xmidb_dr = dram("xmidb_dr", (D, T), BF16)
    inter_dr = dram("inter_dr", (II, T), BF16)

    ctxs = []
    def sbuf(name, shape, dt):
        t = nc.sbuf_tensor(name, list(shape), dt); ctxs.append(t); return t.__enter__()
    def psum(name, shape, dt):
        t = nc.psum_tensor(name, list(shape), dt); ctxs.append(t); return t.__enter__()

    AB = sbuf("AB", [128, 65536], BF16)
    APLr = AB[:, 0:24576].rearrange("p (c q) -> p c q", q=24)
    XY = AB[:, 24576:57344]
    def xyc(c):
        return XY[:, c * 2048:(c + 1) * 2048]
    ABi = AB.rearrange("p (ic t) -> p ic t", t=1024)
    XYTOK = tuple(f"xy{c}" for c in range(KC))
    APTOK = tuple(f"apl{q}" for q in range(24))
    ABTOK = tuple(f"abg{g}" for g in range(8))

    ARENA = sbuf("ARENA", [128, 8448], BF16)
    ARTOK = ("AR0", "AR1", "AR2", "AR3")
    MRB = sbuf("MRB", [128, 4096], F16)
    FT = sbuf("FT", [128, 6144], F16)
    MKT = sbuf("MKT", [128, 1024], U8)
    GST = sbuf("GST", [128, 4096], BF16)
    WRK = sbuf("WRK", [128, 1216], F32)
    SMA = sbuf("SMA", [128, 384], F32)
    SMB = sbuf("SMB", [128, 1024], BF16)
    EVC = sbuf("EVC", [128, 6144], BF16)
    XFT = sbuf("XFT", [128, 2048], F32)
    T5Q = XFT[:, 0:1280]
    BT = sbuf("BT", [128, 512], BF16)
    BNC = sbuf("BNC", [128, 2048], BF16)

    pW = [psum(f"pW{i}", [128, 512], F32) for i in range(4)]
    pacc = [psum(f"pA{i}", [128, 512], F32) for i in range(2)]
    pQc = psum("pQc", [128, 512], F32)
    pXs = psum("pXs", [128, 512], F16)

    DMASEMS = ("sq", "sqG", "sqG1", "sIN0", "sIN1", "sWS0", "sWS1", "sWB2", "sFX", "sOT", "sBD", "sC", "sWB0", "sWB1", "sWB2c", "sWB3", "sT5", "sF0",
               "sF1", "sMK", "sLS0", "sLS1", "sXT", "sWSL", "sYG", "sFA0",
               "sFA1", "sFBI", "sFBD", "sSG", "sDBG")
    sems = {}
    for nm in DMASEMS + ("sv", "ss", "st", "sp"):
        t = nc.semaphore(nm); ctxs.append(t)
        sems[nm] = _Sem(t.__enter__(), nm)
    B = _B(sems, set(DMASEMS))

    def dma(q, sem, out, in_, w=(), r=()):
        return B.op(q, lambda e, o=out, i=in_: e.dma_start(out=o, in_=i), sem, 16, w=w, r=r)
    def dmaT(q, sem, out, in_, w=(), r=()):
        return B.op(q, lambda e, o=out, i=in_: e.dma_start_transpose(o, i), sem, 16, w=w, r=r)

    eye_s = SMB[:, 0:128]; ones_s = SMB[:, 128:256]
    st_s = SMB[:, 256:512]
    wpb_s = [SMB[:, 512 + 16 * i:512 + 16 * (i + 1)] for i in range(2)]
    mub_s = [SMB[:, 544 + 2 + i:544 + 3 + i] for i in range(2)]
    tB2 = None
    onescol_s = SMA[0:1, 0:128]
    beta_s = SMA[0:1, 128:144]
    jrev_s = SMA[:, 160:176]
    lnw_s = SMA[:, 176:192]; lnb_s = SMA[:, 192:208]
    lam_s = SMA[:, 208:224]; lnl_s = SMA[:, 224:240]
    pmu_s = SMA[:, 240:256]; pr_s = SMA[:, 256:272]
    carry_s = SMA[:, 288:304]
    wp_s = SMA[:, 304:320]
    mu1 = MRB[:, 0:2048]; r1 = MRB[:, 2048:4096]
    eps_c = SMA[:, 320:321]; mhalf_c = SMA[:, 321:322]
    tmp0 = WRK[:, 0:512]; tmp1 = WRK[:, 512:1024]
    asc_s = WRK[:, 1024:1216]
    tb_ = [BT[:, 0:128], BT[:, 128:256], BT[:, 256:384], BT[:, 384:512],
           SMB[:, 640:768], SMB[:, 768:896]]
    qh = EVC[:, 0:2048]; vh = EVC[:, 2048:4096]; syh = EVC[:, 4096:6144]
    spbv = EVC[0:16, 2048:4096]
    wsl_ = [ARENA[:, 0:2048], ARENA[:, 2048:4096]]
    EQ = ("Eq0", "Eq1"); EV = ("Ev0", "Ev1"); ES = ("Es0", "Es1")

    B.op("vector", lambda e: e.memset(eps_c, LN_EPS), "sv", w=("epsc",))
    B.op("vector", lambda e: e.memset(mhalf_c, -0.5), "sv", w=("epsc",))
    dma("sync", "sC", eye_s, eye, w=("eye",))
    dma("sync", "sC", ones_s, ones128, w=("ones",))
    dma("sync", "sC", onescol_s, onescol, w=("onescol",))
    dma("sync", "sC", beta_s, beta, w=("beta",))
    dma("sync", "sC", jrev_s, jrev, w=("jrev",))
    dma("sync", "sC", lnw_s, lnww, w=("lnw",))
    dma("sync", "sC", lnb_s, lnbb, w=("lnb",))

    B.rd("tensor", "onescol", "beta")
    B.op("tensor", lambda e: e.matmul(pacc[0][:, 0:H], onescol_s, beta_s,
                                      start=True, stop=True), "st", w=("pacc0",))
    B.rd("scalar", "pacc0")
    B.op("scalar", lambda e: e.activation(lam_s, pacc[0][:, 0:H], AF.Sigmoid),
         "ss", w=("lam",), r=("pacc0",))
    B.rd("vector", "lam")
    B.op("vector", lambda e: e.tensor_single_scalar(lnl_s, lam_s, 1e-38, ALU.max),
         "sv", w=("lnl",))
    B.op("scalar", lambda e: e.activation(lnl_s, lnl_s, AF.Ln), "ss",
         w=("lnl",), r=("lnl",))

    ACOEF = [[(1, 1.0)],
             [(0, -0.5), (2, 0.5)],
             [(0, 1.0), (1, -2.5), (2, 2.0), (3, -0.5)],
             [(0, -0.5), (1, 1.5), (2, -1.5), (3, 0.5)]]
    NQ = NT // 4
    for mi in range(3):
        for qb in range(4):
            c0 = qb * NQ
            t5v = [T5Q[:, j * NQ:(j + 1) * NQ] for j in range(5)]
            for j in range(5):
                B.war("sync", "XF0", "XF1")
                dma("sync", "sT5", t5v[j], t5in[mi][j, :, c0:c0 + NQ],
                    w=("XF0", "XF1") if j == 4 else ())
            for part in range(2):
                for q in range(4):
                    terms = [(tap + part, co) for tap, co in ACOEF[q]]
                    dq = mi * 8 + part * 4 + q
                    dst = APLr[:, c0:c0 + NQ, dq]
                    for j, (tap, co) in enumerate(terms):
                        last = (j == len(terms) - 1)
                        o = dst if last else (tmp0[:, 0:NQ] if j % 2 == 0 else tmp1[:, 0:NQ])
                        wtok = (f"apl{dq}",) if last else ("t0" if j % 2 == 0 else "t1",)
                        B.war("vector", *wtok)
                        if j == 0:
                            B.op("vector", lambda e, o=o, i=t5v[tap], co=co:
                                 e.tensor_single_scalar(o, i, co, ALU.mult),
                                 "sv", w=wtok, r=("XF0",))
                        else:
                            prev = tmp0[:, 0:NQ] if (j - 1) % 2 == 0 else tmp1[:, 0:NQ]
                            ptok = "t0" if (j - 1) % 2 == 0 else "t1"
                            B.rd("vector", ptok)
                            B.op("vector", lambda e, o=o, i=t5v[tap], co=co, pv=prev:
                                 e.scalar_tensor_tensor(out=o, in0=i, scalar=co, in1=pv,
                                                        op0=ALU.mult, op1=ALU.add),
                                 "sv", w=wtok, r=("XF0", ptok))

    _wsem = ("sWB0", "sWB1", "sWB2c", "sWB3")
    _cast_state = {"k": 0}
    _cast_order = [(wi, c8) for wi in (1, 0, 2, 3) for c8 in range(4)]
    def cast_pump(n):
        k = _cast_state["k"]
        for j in range(k, min(k + n, 16)):
            wi, c8 = _cast_order[j]
            dma("gpsimd", _wsem[wi], w_b[wi][512 * c8:512 * (c8 + 1), :],
                wr_[wi][512 * c8:512 * (c8 + 1), :],
                w=(f"wb{wi}",) if c8 == 3 else ())
        _cast_state["k"] = min(k + n, 16)

    def build_wT(wi, c0, nc_):
        B.rd("scalar", f"wb{wi}")
        for c in range(c0, c0 + nc_):
            for hf in range(2):
                buf = BNC[:, hf * 1024:hf * 1024 + 1024]
                B.war("scalar", f"bnc{hf}")
                B.op("scalar", lambda e, o=buf,
                     i=w_b[wi][1024 * hf:1024 * (hf + 1), c * 128:(c + 1) * 128]:
                     e.dma_start_transpose(o, i), "sWB2", 16, w=(f"bnc{hf}",))
                B.rd("scalar", f"bnc{hf}")
                B.op("scalar", lambda e,
                     o=w_bT[wi][c * 128:(c + 1) * 128, 1024 * hf:1024 * (hf + 1)],
                     i=buf:
                     e.dma_start(out=o, in_=i), "sWB2", 16,
                     r=(f"bnc{hf}",), w=(f"wT{wi}",))

    def flat4(t):
        return t.rearrange("(s tj p j) -> s p tj j", tj=8, p=128, j=128)
    f1f = flat4(f1t); f2f = flat4(f2t); f3f = flat4(f3t); mkf = flat4(mskt)
    gmats = (g_dn, g_gate, g_up)
    gmi = (2, 0, 1)
    gfl = [flat4(g.rearrange("a b -> (a b)")) for g in gmats]
    gw_cnt = [dict() for _ in range(3)]

    def blend_supertile(s):
        bi = s % 2
        f1b = FT[:, bi * 1024:bi * 1024 + 1024]
        f2b = FT[:, 2048 + bi * 1024:2048 + bi * 1024 + 1024]
        f3b = FT[:, 4096 + bi * 1024:4096 + bi * 1024 + 1024]
        mkb = MKT[:, 0:1024]
        fv = lambda b, tj: b.rearrange("p (tj j) -> p tj j", j=128)[:, tj, :]
        B.war("gpsimd", f"fb1{bi}", f"fb2{bi}", f"fb3{bi}")
        dma("gpsimd", "sF0" if bi == 0 else "sF1", f1b.rearrange("p (tj j) -> p tj j", j=128), f1f[s], w=(f"fb1{bi}",))
        dma("gpsimd", "sF0" if bi == 0 else "sF1", f2b.rearrange("p (tj j) -> p tj j", j=128), f2f[s], w=(f"fb2{bi}",))
        dma("gpsimd", "sF0" if bi == 0 else "sF1", f3b.rearrange("p (tj j) -> p tj j", j=128), f3f[s], w=(f"fb3{bi}",))
        B.war("gpsimd", "mk")
        dma("gpsimd", "sMK", mkb.rearrange("p (tj j) -> p tj j", j=128), mkf[s], w=("mk",))
        for m in range(3):
            mi = gmi[m]
            asc = asc_s[:, m * 64:(m + 1) * 64]
            src = APLr[:, 8 * s:8 * s + 8, mi * 8:mi * 8 + 8]
            B.op("scalar", lambda e, a=asc, sr=src: e.copy(out=a, in_=sr),
                 "ss", w=(f"asc{m}",),
                 r=tuple(f"apl{mi * 8 + q}" for q in range(8)))
        def emit_s(tj):
            fja = fv(f1b, tj)
            for m in range(3):
                asc = asc_s[:, m * 64:(m + 1) * 64]
                a = [asc[:, tj * 8 + q:tj * 8 + q + 1] for q in range(8)]
                slot = (bi if m == 0 else (2 + bi if m == 1 else 1 - bi))
                gs = GST[:, slot * 1024 + tj * 128:slot * 1024 + tj * 128 + 128]
                gtok = f"gs{slot}_{tj}"
                fr1 = (f"fb1{bi}", f"asc{m}")
                pp = (s * 24 + tj * 3 + m) % 6
                tb = tb_[pp]
                tbt = f"tb{pp}"
                B.op("scalar", lambda e, o=gs, i=fja, s1=a[1], s2=a[0]:
                     e.activation(o, i, AF.Identity, scale=s1, bias=s2),
                     "ss", w=(gtok,), r=fr1)
                B.op("scalar", lambda e, o=tb, i=fja, s1=a[5], s2=a[4]:
                     e.activation(o, i, AF.Identity, scale=s1, bias=s2),
                     "ss", w=(tbt,), r=fr1)

        def emit_v(tj):
            fjb = fv(f2b, tj); fjc = fv(f3b, tj)
            mkj = fv(mkb, tj)
            for m in range(3):
                asc = asc_s[:, m * 64:(m + 1) * 64]
                a = [asc[:, tj * 8 + q:tj * 8 + q + 1] for q in range(8)]
                slot = (bi if m == 0 else (2 + bi if m == 1 else 1 - bi))
                gs = GST[:, slot * 1024 + tj * 128:slot * 1024 + tj * 128 + 128]
                gtok = f"gs{slot}_{tj}"
                fr2 = (f"fb2{bi}", f"asc{m}")
                fr3 = (f"fb3{bi}", f"asc{m}")
                pp = (s * 24 + tj * 3 + m) % 6
                tb = tb_[pp]
                tbt = f"tb{pp}"
                B.op("vector", lambda e, o=gs, i=fjb, s1=a[2]:
                     e.scalar_tensor_tensor(out=o, in0=i, scalar=s1, in1=o,
                                            op0=ALU.mult, op1=ALU.add), "sv",
                     w=(gtok,), r=fr2 + (gtok,))
                B.op("vector", lambda e, o=gs, i=fjc, s1=a[3]:
                     e.scalar_tensor_tensor(out=o, in0=i, scalar=s1, in1=o,
                                            op0=ALU.mult, op1=ALU.add), "sv",
                     w=(gtok,), r=(gtok,) + fr3)
                B.op("vector", lambda e, o=tb, i=fjb, s1=a[6]:
                     e.scalar_tensor_tensor(out=o, in0=i, scalar=s1, in1=o,
                                            op0=ALU.mult, op1=ALU.add), "sv",
                     w=(tbt,), r=fr2 + (tbt,))
                B.op("vector", lambda e, o=tb, i=fjc, s1=a[7]:
                     e.scalar_tensor_tensor(out=o, in0=i, scalar=s1, in1=o,
                                            op0=ALU.mult, op1=ALU.add), "sv",
                     w=(tbt,), r=fr3 + (tbt,))
                B.op("vector", lambda e, o=gs, mk=mkj, d=tb:
                     e.copy_predicated(out=o, mask=mk, data=d), "sv",
                     w=(gtok,), r=(tbt, "mk"))

        emit_s(0)
        for tj in range(1, 8):
            emit_s(tj)
            emit_v(tj - 1)
        emit_v(7)
        for m in range(3):
            slot = (bi if m == 0 else (2 + bi if m == 1 else 1 - bi))
            gtoks = tuple(f"gs{slot}_{tj}" for tj in range(8))
            B.rd("sync", *gtoks)
            gs = GST[:, slot * 1024:slot * 1024 + 1024]
            c = dma("sync", "sqG" if s % 2 == 0 else "sqG1", gfl[m][s],
                    gs.rearrange("p (tj j) -> p tj j", j=128),
                    r=gtoks, w=(f"G{m}",) + gtoks)
            gw_cnt[m][s] = c

    def bounce_gu_kilo(k):
        for m, gsrc, gdst in ((1, g_gate, gT_gate), (2, g_up, gT_up)):
            assert (16 * k + 15) in gw_cnt[m]
            B._wait("sync", "sqG", sems["sqG"].cnt)
            B._wait("sync", "sqG1", sems["sqG1"].cnt)
            for p0 in range(16):
                bi = p0 % 2
                buf = BNC[:, bi * 1024:bi * 1024 + 1024]
                B.war("sync", f"bnc{bi}")
                dmaT("sync", "sqG", buf,
                     gsrc[1024 * k:1024 * (k + 1), 128 * p0:128 * (p0 + 1)],
                     w=(f"bnc{bi}",))
                B.rd("sync", f"bnc{bi}")
                dma("sync", "sqG",
                    gdst[128 * p0:128 * (p0 + 1), 1024 * k:1024 * (k + 1)],
                    buf, r=(f"bnc{bi}",), w=(f"gT{m}k{k}",))

    def bounce_dn_pairs(p0, n):
        if p0 == 0:
            assert (NSUP - 1) in gw_cnt[0]
            B._wait("scalar", "sqG", sems["sqG"].cnt)
            B._wait("scalar", "sqG1", sems["sqG1"].cnt)
        for p in range(p0, p0 + n):
            ic = p // 2; hf = p % 2
            buf = BNC[:, hf * 1024:hf * 1024 + 1024]
            B.war("scalar", f"bnc{hf}")
            B.op("scalar", lambda e, o=buf, i=g_dn[1024 * hf:1024 * (hf + 1),
                                                   128 * ic:128 * (ic + 1)]:
                 e.dma_start_transpose(o, i), "sBD", 16, w=(f"bnc{hf}",))
            B.rd("scalar", f"bnc{hf}")
            B.op("scalar", lambda e, o=gT_dn[128 * ic:128 * (ic + 1),
                                             1024 * hf:1024 * (hf + 1)], i=buf:
                 e.dma_start(out=o, in_=i), "sBD", 16, r=(f"bnc{hf}",), w=("gT0",))

    _pump = {"next": 0}
    def pump(n):
        s0 = _pump["next"]
        for s in range(s0, min(s0 + n, NSUP)):
            blend_supertile(s)
            if s % 16 == 15:
                bounce_gu_kilo(s // 16)
        _pump["next"] = min(s0 + n, NSUP)

    def ln_stats(src_dr, tag, bf16src):
        for half in range(2):
            for c in range(KC):
                sl = c % 2
                src = src_dr[c * 128:(c + 1) * 128, half * 1024:(half + 1) * 1024]
                xb = EVC[:, sl * 1024:sl * 1024 + 1024]
                if bf16src:
                    B.war("sync", EQ[sl])
                    dma("sync", "sLS0" if sl == 0 else "sLS1", xb, src, w=(EQ[sl],))
                else:
                    xf = XFT[:, sl * 1024:sl * 1024 + 1024]
                    B.war("sync", f"XF{sl}")
                    dma("sync", "sLS0" if sl == 0 else "sLS1", xf, src, w=(f"XF{sl}",))
                    B.rd("gpsimd", f"XF{sl}")
                    B.war("gpsimd", EQ[sl])
                    B.op("gpsimd", lambda e, o=xb, i=xf: e.tensor_copy(out=o, in_=i),
                         "sp", w=(EQ[sl],), r=(f"XF{sl}",))
                xq = EVC[:, 2048 + sl * 1024:2048 + sl * 1024 + 1024]
                B.rd("scalar", EQ[sl])
                B.war("scalar", EV[sl])
                B.op("scalar", lambda e, o=xq, i=xb: e.activation(o, i, AF.Square),
                     "ss", w=(EV[sl],), r=(EQ[sl],))
                st0 = (c == 0); sp0 = (c == KC - 1)
                B.rd("tensor", "ones", EQ[sl], EV[sl])
                if st0:
                    B.war("tensor", "pW0", "pW1", "pW2", "pW3")
                for j in range(2):
                    fn1 = lambda e, p=pW[j], i=xb[:, j * 512:(j + 1) * 512], s0=st0, s1=sp0: \
                        e.matmul(p[:], ones_s, i, start=s0, stop=s1)
                    fn2 = lambda e, p=pW[2 + j], i=xq[:, j * 512:(j + 1) * 512], s0=st0, s1=sp0: \
                        e.matmul(p[:], ones_s, i, start=s0, stop=s1)
                    if sp0:
                        B.op("tensor", fn1, "st", w=(f"pW{j}",), r=(EQ[sl],))
                        B.op("tensor", fn2, "st", w=(f"pW{2 + j}",), r=(EV[sl],))
                    elif j == 1:
                        B.plain("tensor", lambda e, f=fn1: f(e))
                        B.op("tensor", fn2, "st", r=(EQ[sl], EV[sl]))
                    else:
                        B.plain("tensor", lambda e, f=fn1: f(e))
                        B.plain("tensor", lambda e, f=fn2: f(e))
            for j in range(2):
                tok0 = half * 1024 + j * 512
                mu_sl = mu1[:, tok0:tok0 + 512]; r_sl = r1[:, tok0:tok0 + 512]
                B.rd("vector", f"pW{j}", f"pW{2 + j}")
                B.war("vector", "t0", "t1", "mu_c")
                B.op("vector", lambda e, p=pW[j]:
                     e.tensor_single_scalar(tmp0, p[:], 1.0 / D, ALU.mult),
                     "sv", w=("t0",), r=(f"pW{j}",))
                B.op("vector", lambda e, m=mu_sl: e.tensor_copy(out=m, in_=tmp0),
                     "sv", w=("mu_c",), r=("t0",))
                B.op("vector", lambda e: e.tensor_mul(tmp1, tmp0, tmp0),
                     "sv", w=("t1",), r=("t0",))
                B.op("vector", lambda e, p=pW[2 + j]:
                     e.scalar_tensor_tensor(out=tmp0, in0=p[:], scalar=1.0 / D,
                                            in1=tmp1, op0=ALU.mult, op1=ALU.subtract),
                     "sv", w=("t0",), r=(f"pW{2 + j}", "t1"))
                B.rd("scalar", "t0", "epsc")
                B.war("scalar", "r_c")
                B.op("scalar", lambda e: e.activation(tmp0, tmp0, AF.Ln, bias=eps_c),
                     "ss", w=("t0",), r=("t0", "epsc"))
                B.op("scalar", lambda e, r=r_sl: e.activation(r, tmp0, AF.Exp,
                                                              scale=mhalf_c),
                     "ss", w=("r_c",), r=("t0",))

    for k in range(8):
        pump(1)
        cast_pump(2)
    for wi in (1, 0):
        for c0 in (0, 8):
            pump(1)
            build_wT(wi, c0, 8)
    ln_stats(xT, "l1", False)
    for wi in (2, 3):
        for c0 in (0, 8):
            pump(1)
            build_wT(wi, c0, 8)

    for c in range(KC):
        sl = c % 2
        xfull = XFT[:, 0:2048]
        B.war("sync", "XF0", "XF1")
        dma("sync", "sXT", xfull, xp[c * 128:(c + 1) * 128, :], w=("XF0", "XF1"))
        xb = EVC[:, 0:2048]
        B.rd("gpsimd", "XF0")
        B.war("gpsimd", *EQ)
        B.op("gpsimd", lambda e, o=xb, i=xfull: e.tensor_copy(out=o, in_=i),
             "sp", w=EQ, r=("XF0", "XF1"))
        xq = EVC[:, 2048:4096]
        B.rd("scalar", "XF0")
        B.war("scalar", *EV)
        B.op("scalar", lambda e, o=xq, i=xfull: e.activation(o, i, AF.Square),
             "ss", w=EV, r=("XF0", "XF1"))
        B.rd("vector", "XF0", *EV)
        B.war("vector", "pst0", "pst1", "t0")
        B.op("vector", lambda e, o=pmu_s[:, c:c + 1], i=xfull:
             e.reduce_sum(o, i, axis=AXX), "sv", w=("pst0",), r=("XF0", "XF1"))
        B.op("vector", lambda e, o=pr_s[:, c:c + 1], i=xq:
             e.reduce_sum(o, i, axis=AXX), "sv", w=("pst1",), r=EV)
        B.op("vector", lambda e, o=pmu_s[:, c:c + 1]:
             e.tensor_single_scalar(o, o, 1.0 / D, ALU.mult), "sv", w=("pst0",),
             r=("pst0",))
        B.op("vector", lambda e, o=tmp0[:, 0:1], m=pmu_s[:, c:c + 1]:
             e.tensor_mul(o, m, m), "sv", w=("t0",), r=("pst0",))
        B.op("vector", lambda e, o=pr_s[:, c:c + 1], t=tmp0[:, 0:1]:
             e.scalar_tensor_tensor(out=o, in0=o, scalar=1.0 / D, in1=t,
                                    op0=ALU.mult, op1=ALU.subtract), "sv",
             w=("pst1",), r=("t0", "pst1"))
        B.rd("scalar", "pst1", "epsc")
        B.op("scalar", lambda e, o=pr_s[:, c:c + 1]:
             e.activation(o, o, AF.Ln, bias=eps_c), "ss", w=("pst1",),
             r=("pst1", "epsc"))
        B.op("scalar", lambda e, o=pr_s[:, c:c + 1]:
             e.activation(o, o, AF.Exp, scale=mhalf_c), "ss", w=("pst1",),
             r=("pst1",))
        B.rd("scalar", "lnl", "jrev")
        B.war("scalar", "wp")
        B.op("scalar", lambda e, c_=c: e.activation(wp_s, lnl_s, AF.Exp,
                                                    scale=jrev_s[:, c_:c_ + 1]),
             "ss", w=("wp",))
        B.rd("vector", "wp", "pst1", "pst0")
        B.war("vector", f"wpb{sl}", f"mub{sl}")
        B.op("vector", lambda e, o=wpb_s[sl], c_=c:
             e.tensor_scalar(out=o, in0=wp_s, scalar1=pr_s[:, c_:c_ + 1],
                             scalar2=None, op0=ALU.mult),
             "sv", w=(f"wpb{sl}",), r=("wp", "pst1"))
        B.op("vector", lambda e, o=mub_s[sl], c_=c:
             e.tensor_copy(out=o, in_=pmu_s[:, c_:c_ + 1]),
             "sv", w=(f"mub{sl}",), r=("pst0",))
        st0 = (c == 0); sp0 = (c == KC - 1)
        B.rd("tensor", f"wpb{sl}", f"mub{sl}", *EQ)
        if st0:
            B.war("tensor", "pW0", "pW1", "pW2", "pW3", "pacc0")
        for n in range(4):
            fn = lambda e, p=pW[n], l=wpb_s[sl], i=xb[:, n * 512:(n + 1) * 512], \
                s0=st0, s1=sp0: e.matmul(p[0:16, :], l, i, start=s0, stop=s1)
            if sp0:
                B.op("tensor", fn, "st", w=(f"pW{n}",), r=EQ + (f"wpb{sl}",))
            else:
                B.plain("tensor", lambda e, f=fn: f(e))
        fnm = lambda e, l=wpb_s[sl], i=mub_s[sl], s0=st0, s1=sp0: \
            e.matmul(pacc[0][0:16, 1:2], l, i, start=s0, stop=s1)
        if sp0:
            B.op("tensor", fnm, "st", w=("pacc0",), r=(f"mub{sl}", f"wpb{sl}"))
        else:
            B.op("tensor", fnm, "st", r=(EQ[0], f"wpb{sl}", f"mub{sl}"))
        if c % 4 == 3:
            pump(1)
    B.rd("vector", "pacc0")
    B.war("vector", "t0")
    B.op("vector", lambda e: e.tensor_copy(out=tmp0[0:16, 0:1], in_=pacc[0][0:16, 1:2]),
         "sv", w=("t0",), r=("pacc0",))
    B.war("vector", *EV)
    for n in range(4):
        B.rd("vector", f"pW{n}", "t0")
        B.op("vector", lambda e, n=n, p=pW[n]:
             e.tensor_scalar(out=spbv[:, n * 512:(n + 1) * 512], in0=p[0:16, :],
                             scalar1=tmp0[0:16, 0:1], scalar2=None, op0=ALU.subtract),
             "sv", w=(EV[n // 2],), r=(f"pW{n}", "t0"))
    for c in range(KC):
        B.op("tensor", lambda e, c_=c:
             e.transpose(pXs[:, 0:16], spbv[:, c_ * 128:(c_ + 1) * 128],
                         eye_s[0:16, 0:16]),
             "st", w=("pXs",), r=EV)
        B.op("scalar", lambda e, c_=c:
             e.copy(out=st_s[:, c_ * 16:(c_ + 1) * 16], in_=pXs[:, 0:16]),
             "ss", w=("st_s",), r=("pXs",))

    for c in range(KC):
        xfull = XFT[:, 0:2048]
        B.war("sync", "XF0", "XF1")
        dma("sync", "sXT", xfull, xT[c * 128:(c + 1) * 128, :], w=("XF0", "XF1"))
        x1c = xyc(c)
        B.war("vector", f"xy{c}")
        for n in range(4):
            xtok = "XF0" if n < 2 else "XF1"
            B.rd("vector", xtok, "mu_c", "r_c")
            B.war("vector", "t0")
            B.op("vector", lambda e, i=xfull[:, n * 512:(n + 1) * 512],
                 m=mu1[:, n * 512:(n + 1) * 512]:
                 e.tensor_sub(tmp0, i, m), "sv", w=("t0",), r=(xtok, "mu_c"))
            B.op("vector", lambda e, o=x1c[:, n * 512:(n + 1) * 512],
                 r_=r1[:, n * 512:(n + 1) * 512]:
                 e.tensor_mul(o, tmp0, r_), "sv", w=(f"xy{c}",), r=("t0", "r_c"))
        if c % 2 == 1:
            pump(1)
    if "d_x1" in dbg:
        for c in range(KC):
            B.rd("sync", f"xy{c}")
            dma("sync", "sq", dbg["d_x1"][c * 128:(c + 1) * 128, :], xyc(c),
                r=(f"xy{c}",))

    def w_sweep(widx, head, carry, banks, btoks):
        par = head % 2
        wsl = wsl_[par]
        wslv = wsl.rearrange("p (c a) -> p c a", c=KC)
        B.rd("sync", f"wT{widx}")
        B.war("sync", f"AR{par}")
        src_ap = w_bT[widx].rearrange("(c p) a -> p c a", p=128)[
            :, :, head * 128:(head + 1) * 128]
        dma("sync", "sWS0" if par == 0 else "sWS1", wslv, src_ap,
            w=(f"AR{par}",))
        B.rd("tensor", f"AR{par}", *XYTOK)
        B.war("tensor", *btoks)
        if carry:
            B.war("tensor", "pQc")
            B.rd("tensor", "st_s")
        for c in range(KC):
            st0 = (c == 0); sp0 = (c == KC - 1)
            for n in range(4):
                fn = (lambda e, p=banks[n], wt=wslv[:, c, :],
                      i=xyc(c)[:, n * 512:(n + 1) * 512],
                      s0=st0, s1=sp0: e.matmul(p[:], wt, i, start=s0, stop=s1))
                if sp0:
                    B.op("tensor", fn, "st", w=(btoks[n],), r=(f"AR{par}",))
                elif n == 3:
                    B.op("tensor", fn, "st", r=(f"AR{par}", f"xy{c}"))
                else:
                    B.plain("tensor", lambda e, f=fn: f(e))
            if carry:
                fnc = (lambda e, wt=wslv[:, c, :], c_=c, h_=head, s0=st0, s1=sp0:
                       e.matmul(pQc[:, 0:1], wt,
                                st_s[:, c_ * 16 + h_:c_ * 16 + h_ + 1],
                                start=s0, stop=s1))
                if sp0:
                    B.op("tensor", fnc, "st", w=("pQc",), r=("st_s", f"AR{par}"))
                else:
                    B.op("tensor", fnc, "st", r=("st_s", f"AR{par}"))

    BK_V = (pW[0], pW[1], pW[2], pW[3])
    BT_V = ("pW0", "pW1", "pW2", "pW3")
    BK_Q = (pacc[0], pacc[1], pW[2], pW[3])
    BT_Q = ("pacc0", "pacc1", "pW2", "pW3")
    for head in range(H):
        w_sweep(1, head, True, BK_V, BT_V)
        pump(3)
        for n in range(4):
            B.op("scalar", lambda e, n=n, p=BK_V[n]:
                 e.copy(out=vh[:, n * 512:(n + 1) * 512], in_=p[:]),
                 "ss", w=(EV[n // 2],), r=(BT_V[n],))
        B.op("scalar", lambda e, h_=head:
             e.copy(out=carry_s[:, h_:h_ + 1], in_=pQc[:, 0:1]),
             "ss", w=("carry",), r=("pQc",))
        lam_bc = lam_s[:, head:head + 1].broadcast_to([128, 2048])
        B.op("vector", lambda e, h_=head, lb=lam_bc:
             e.tensor_tensor_scan(out=syh, data0=lb, data1=vh,
                                  initial=carry_s[:, h_:h_ + 1],
                                  op0=ALU.mult, op1=ALU.add),
             "sv", w=ES, r=EV + ES + ("carry", "lam"))
        w_sweep(0, head, False, BK_Q, BT_Q)
        pump(3)
        for n in range(4):
            B.op("scalar", lambda e, n=n, p=BK_Q[n]:
                 e.copy(out=qh[:, n * 512:(n + 1) * 512], in_=p[:]),
                 "ss", w=(EQ[n // 2],), r=(BT_Q[n],))
        if "d_state" in dbg:
            B.rd("sync", *ES)
            dma("sync", "sq", dbg["d_state"][head * 128:(head + 1) * 128, :],
                syh, r=ES)
        B.op("vector", lambda e: e.tensor_mul(syh, qh, syh), "sv",
             w=ES, r=EQ + ES)
        B.rd("sync", *ES)
        dma("sync", "sq", y_dr[head * 128:(head + 1) * 128, :], syh, r=ES,
            w=("y_dr",))
    if "d_carry" in dbg:
        B.rd("sync", "carry")
        dma("sync", "sq", dbg["d_carry"], carry_s, r=("carry",))

    BK_O = (pacc[0], pacc[1], pQc, pW[3])
    BT_O = ("pacc0", "pacc1", "pQc", "pW3")
    for head in range(H):
        bs, bt = (BK_V, BT_V) if head % 2 == 0 else (BK_O, BT_O)
        w_sweep(2, head, False, bs, bt)
        pump(1)
        for n in range(4):
            B.op("scalar", lambda e, n=n, p=bs[n]:
                 e.copy(out=qh[:, n * 512:(n + 1) * 512], in_=p[:]),
                 "ss", w=(EQ[n // 2],), r=(bt[n],))
        B.rd("sync", *EQ)
        dma("sync", "sq", gg_dr[head * 128:(head + 1) * 128, :], qh, r=EQ,
            w=("gg_dr",))
    if "d_g" in dbg:
        B.rd("sync", "gg_dr")
        for c in range(KC):
            dma("sync", "sq", dbg["d_g"][c * 128:(c + 1) * 128, :],
                gg_dr[c * 128:(c + 1) * 128, :])
    if "d_y" in dbg:
        B.rd("sync", "y_dr")
        for c in range(KC):
            dma("sync", "sq", dbg["d_y"][c * 128:(c + 1) * 128, :],
                y_dr[c * 128:(c + 1) * 128, :])

    B.rd("sync", "y_dr")
    ln_stats(y_dr, "ly", True)
    pump(4)

    for c in range(KC):
        B.war("sync", *EQ)
        B.rd("sync", "y_dr")
        dma("sync", "sYG", qh, y_dr[c * 128:(c + 1) * 128, :], w=EQ)
        B.war("sync", *EV)
        B.rd("sync", "gg_dr")
        dma("sync", "sYG", vh, gg_dr[c * 128:(c + 1) * 128, :], w=EV)
        B.rd("scalar", *EV)
        B.war("scalar", *ES)
        B.op("scalar", lambda e: e.activation(syh, vh, AF.Sigmoid), "ss",
             w=ES, r=EV)
        y2c = xyc(c)
        B.war("vector", f"xy{c}")
        for n in range(4):
            sl_ = slice(n * 512, (n + 1) * 512)
            B.rd("vector", *EQ)
            B.rd("vector", *ES)
            B.rd("vector", "mu_c", "r_c", "lnw", "lnb")
            B.war("vector", "t0")
            B.op("vector", lambda e, i=qh[:, sl_], m=mu1[:, sl_]:
                 e.tensor_sub(tmp0, i, m), "sv", w=("t0",), r=EQ + ("mu_c",))
            B.op("vector", lambda e, r_=r1[:, sl_]:
                 e.tensor_mul(tmp0, tmp0, r_), "sv", w=("t0",), r=("r_c", "t0"))
            B.op("vector", lambda e, c_=c:
                 e.tensor_scalar(out=tmp0, in0=tmp0, scalar1=lnw_s[:, c_:c_ + 1],
                                 scalar2=lnb_s[:, c_:c_ + 1], op0=ALU.mult,
                                 op1=ALU.add), "sv", w=("t0",), r=("t0", "lnw", "lnb"))
            B.op("vector", lambda e, s=syh[:, sl_]:
                 e.tensor_mul(tmp0, tmp0, s), "sv", w=("t0",), r=ES + ("t0",))
            B.op("vector", lambda e, o=y2c[:, sl_], g=vh[:, sl_]:
                 e.tensor_mul(o, tmp0, g), "sv", w=(f"xy{c}",), r=("t0",) + EV)
        pump(1)

    for head in range(H):
        bs, bt = (BK_V, BT_V) if head % 2 == 0 else (BK_O, BT_O)
        w_sweep(3, head, False, bs, bt)
        pump(1)
        xfull = XFT[:, 0:2048]
        B.war("sync", "XF0", "XF1")
        dma("sync", "sXT", xfull, xT[head * 128:(head + 1) * 128, :], w=("XF0", "XF1"))
        for n in range(4):
            xtok = "XF0" if n < 2 else "XF1"
            B.rd("vector", bt[n], xtok)
            B.op("vector", lambda e, p=bs[n], xs=xfull[:, n * 512:(n + 1) * 512]:
                 e.tensor_add(xs, p[:], xs),
                 "sv", w=("XF0", "XF1") if n == 3 else (xtok,), r=(bt[n],))
        B.rd("sync", "XF0", "XF1")
        dma("sync", "sq", xmid_dr[head * 128:(head + 1) * 128, :], xfull,
            r=("XF0", "XF1"), w=("xmid",))
        B.rd("gpsimd", "XF0", "XF1")
        dma("gpsimd", "sSG", xmidb_dr[head * 128:(head + 1) * 128, :], xfull,
            r=("XF0", "XF1"), w=("xmidb",))
        pump(1)
    if "d_xmid" in dbg:
        B.rd("sync", "xmid")
        for c in range(KC):
            dma("sync", "sq", dbg["d_xmid"][c * 128:(c + 1) * 128, :],
                xmid_dr[c * 128:(c + 1) * 128, :])

    B.rd("sync", "xmidb")
    ln_stats(xmidb_dr, "l2", True)
    pump(4)
    for c in range(KC):
        B.war("sync", *EQ)
        dma("sync", "sYG", qh, xmidb_dr[c * 128:(c + 1) * 128, :], w=EQ)
        x2c = xyc(c)
        B.war("vector", f"xy{c}")
        for n in range(4):
            sl_ = slice(n * 512, (n + 1) * 512)
            B.rd("vector", *EQ)
            B.rd("vector", "mu_c", "r_c")
            B.war("vector", "t0")
            B.op("vector", lambda e, i=qh[:, sl_], m=mu1[:, sl_]:
                 e.tensor_sub(tmp0, i, m), "sv", w=("t0",), r=EQ + ("mu_c",))
            B.op("vector", lambda e, o=x2c[:, sl_], r_=r1[:, sl_]:
                 e.tensor_mul(o, tmp0, r_), "sv", w=(f"xy{c}",), r=("t0", "r_c"))
        pump(1)
    if "d_x2" in dbg:
        for c in range(KC):
            B.rd("sync", f"xy{c}")
            dma("sync", "sq", dbg["d_x2"][c * 128:(c + 1) * 128, :], xyc(c),
                r=(f"xy{c}",))
    pump(NSUP)

    gTgr = gT_gate.rearrange("(a s) i -> s a i", s=128)
    gTur = gT_up.rearrange("(a s) i -> s a i", s=128)
    sgv_ = [EVC[:, 0:512], EVC[:, 2048:2560]]
    gcv_ = [EVC[:, 512:1024], EVC[:, 2560:3072]]
    ttv_ = [EVC[:, 1024:1536], EVC[:, 3072:3584]]
    itv_ = [EVC[:, 1536:2048], EVC[:, 3584:4096]]
    for rng in range(64):
        par = rng % 2
        gsl = ARENA[:, par * 2048:par * 2048 + 2048]
        usl = ARENA[:, 4096 + par * 2048:4096 + par * 2048 + 2048]
        gslv = gsl.rearrange("p (a i) -> p a i", a=16)
        uslv = usl.rearrange("p (a i) -> p a i", a=16)
        gt = ARTOK[par]; ut = ARTOK[2 + par]
        B.war("sync", gt)
        B.rd("sync", f"gT1k{rng // 8}")
        dma("sync", "sFA0" if par == 0 else "sFA1", gslv, gTgr[:, :, rng * 128:(rng + 1) * 128], w=(gt,))
        B.war("sync", ut)
        B.rd("sync", f"gT2k{rng // 8}")
        dma("sync", "sFA0" if par == 0 else "sFA1", uslv, gTur[:, :, rng * 128:(rng + 1) * 128], w=(ut,))
        for n in range(4):
            pgi = (2 * n) % 4; pui = (2 * n + 1) % 4
            B.rd("tensor", gt, ut, *XYTOK)
            B.war("tensor", f"pW{pgi}", f"pW{pui}")
            for p0 in range(KC):
                st0 = (p0 == 0); sp0 = (p0 == KC - 1)
                fn1 = lambda e, p=pW[pgi], l=gslv[:, p0, :], \
                    i=xyc(p0)[:, n * 512:(n + 1) * 512], s0=st0, s1=sp0: \
                    e.matmul(p[:], l, i, start=s0, stop=s1)
                fn2 = lambda e, p=pW[pui], l=uslv[:, p0, :], \
                    i=xyc(p0)[:, n * 512:(n + 1) * 512], s0=st0, s1=sp0: \
                    e.matmul(p[:], l, i, start=s0, stop=s1)
                if sp0:
                    B.op("tensor", fn1, "st", w=(f"pW{pgi}",),
                         r=(gt,) + XYTOK)
                    B.op("tensor", fn2, "st", w=(f"pW{pui}",), r=(ut,))
                else:
                    B.plain("tensor", lambda e, f=fn1: f(e))
                    B.plain("tensor", lambda e, f=fn2: f(e))
            np_ = n % 2
            sgv = sgv_[np_]; gcv = gcv_[np_]; ttv = ttv_[np_]; itv = itv_[np_]
            et0 = EQ[0] if np_ == 0 else EV[0]
            et1 = EQ[1] if np_ == 0 else EV[1]
            B.op("scalar", lambda e, p=pW[pgi], o=sgv: e.activation(o, p[:], AF.Sigmoid),
                 "ss", w=(et0,), r=(f"pW{pgi}",))
            B.op("scalar", lambda e, p=pW[pgi], o=gcv: e.copy(out=o, in_=p[:]),
                 "ss", w=(et0,), r=(f"pW{pgi}",))
            B.op("vector", lambda e, o=ttv, a_=sgv, b_=gcv: e.tensor_mul(o, a_, b_),
                 "sv", w=(et1,), r=(et0,))
            B.op("vector", lambda e, p=pW[pui], o=itv, t=ttv: e.tensor_mul(o, t, p[:]),
                 "sv", w=(et1,), r=(f"pW{pui}", et1))
            dma("gpsimd", "sIN0" if np_ == 0 else "sIN1",
                inter_dr[rng * 128:(rng + 1) * 128,
                         n * 512:(n + 1) * 512], itv,
                r=(et1,), w=("inter",))
        if rng >= 32:
            bounce_dn_pairs((rng - 32) * 4, 4)
    if "d_inter" in dbg:
        B.rd("sync", "inter")
        for c in range(II // 128):
            dma("sync", "sq", dbg["d_inter"][c * 128:(c + 1) * 128, :],
                inter_dr[c * 128:(c + 1) * 128, :])

    if "d_gw" in dbg:
        B.rd("sync", "G1")
        for c in range(II // 128):
            dma("sync", "sq", dbg["d_gw"][c * 128:(c + 1) * 128, :],
                g_gate[c * 128:(c + 1) * 128, :])
    if "d_dw" in dbg:
        B.rd("sync", "G0")
        for c in range(KC):
            dma("sync", "sq", dbg["d_dw"][c * 128:(c + 1) * 128, :],
                g_dn[c * 128:(c + 1) * 128, :])

    gTdr = gT_dn.rearrange("(kc s) d -> s kc d", s=128)
    for half in range(2):
        B.war("sync", *(ABTOK + XYTOK + APTOK + ("asc0", "asc1", "asc2")))
        B._wait("sync", "sIN0", sems["sIN0"].cnt)
        B._wait("sync", "sIN1", sems["sIN1"].cnt)
        for grp in range(8):
            src = inter_dr[grp * 1024:(grp + 1) * 1024,
                           half * 1024:(half + 1) * 1024]
            dma("sync", "sFBI", ABi[:, grp * 8:(grp + 1) * 8, :],
                src.rearrange("(ic s) t -> s ic t", s=128), w=(f"abg{grp}",))
        for do in range(KC):
            dp = do % 2
            bk = (pW[0], pW[1]) if dp == 0 else (pW[2], pW[3])
            bt = ("pW0", "pW1") if dp == 0 else ("pW2", "pW3")
            dsl = ARENA[:, 0:8192] if dp == 0 else None
            dsl = ARENA[:, 0:8192]
            dslv = dsl.rearrange("p (kc d) -> p kc d", kc=64)
            B.war("sync", *ARTOK)
            B.rd("sync", "gT0")
            dma("sync", "sFBD", dslv, gTdr[:, :, do * 128:(do + 1) * 128], w=ARTOK)
            B.rd("tensor", *ARTOK)
            B.rd("tensor", *ABTOK)
            B.war("tensor", *bt)
            for kc in range(64):
                st0 = (kc == 0); sp0 = (kc == 63)
                for n in range(2):
                    fn = (lambda e, p=bk[n], l=dslv[:, kc, :],
                          i=ABi[:, kc, n * 512:(n + 1) * 512], s0=st0, s1=sp0:
                          e.matmul(p[:], l, i, start=s0, stop=s1))
                    if sp0:
                        B.op("tensor", fn, "st", w=(bt[n],),
                             r=ARTOK + ABTOK)
                    else:
                        B.plain("tensor", lambda e, f=fn: f(e))
            xfull = XFT[:, 0:1024]
            B.war("gpsimd", "XF0")
            dma("gpsimd", "sFX", xfull,
                xmid_dr[do * 128:(do + 1) * 128, half * 1024:(half + 1) * 1024],
                w=("XF0",))
            for n in range(2):
                B.rd("vector", bt[n], "XF0")
                B.op("vector", lambda e, p=bk[n], xs=xfull[:, n * 512:(n + 1) * 512]:
                     e.tensor_add(xs, p[:], xs),
                     "sv", w=("XF0",), r=(bt[n],))
            B.rd("gpsimd", "XF0")
            dma("gpsimd", "sOT", oT[do * 128:(do + 1) * 128,
                                    half * 1024:(half + 1) * 1024],
                xfull, r=("XF0",), w=("oT",))

    assert _pump["next"] == NSUP, _pump["next"]
    for snm in ("sq", "sqG", "sqG1", "sIN0", "sIN1", "sBD", "sSG", "sOT",
                "sDBG"):
        if sems[snm].cnt:
            B.streams["sync"].append(
                lambda e, h=sems[snm].h, c=sems[snm].cnt: e.wait_ge(h, c))

    with nc.Block() as block:
        @block.sync
        def _(eng):
            for fn in B.streams["sync"]:
                fn(eng)
        @block.gpsimd
        def _(eng):
            for fn in B.streams["gpsimd"]:
                fn(eng)
        @block.vector
        def _(eng):
            for fn in B.streams["vector"]:
                fn(eng)
        @block.scalar
        def _(eng):
            for fn in B.streams["scalar"]:
                fn(eng)
        @block.tensor
        def _(eng):
            for fn in B.streams["tensor"]:
                fn(eng)
    for c in reversed(ctxs):
        c.__exit__(None, None, None)
    _NC_CACHE[key] = nc
    return nc

def kernel(L0_attn_x, L0_attn_Wq, L0_attn_Wv, L0_attn_Wg, L0_attn_Wo,
           L0_attn_beta, L0_attn_ln_w, L0_attn_ln_b,
           L0_ffn_gate_cp, L0_ffn_up_cp, L0_ffn_down_cp, _debug=()):
    tabs = _host_tables()
    x = np.ascontiguousarray(np.asarray(L0_attn_x, dtype=np.float32))
    assert x.shape == (4, 4096, 2048)
    t5g = _t5_planes(L0_ffn_gate_cp)
    t5u = _t5_planes(L0_ffn_up_cp)
    t5d = _t5_planes(L0_ffn_down_cp)
    lnw = np.ascontiguousarray(np.asarray(L0_attn_ln_w, np.float32).reshape(KC, 128).T)
    lnb = np.ascontiguousarray(np.asarray(L0_attn_ln_b, np.float32).reshape(KC, 128).T)
    shared = dict(
        wq=np.ascontiguousarray(np.asarray(L0_attn_Wq, np.float32)),
        wv=np.ascontiguousarray(np.asarray(L0_attn_Wv, np.float32)),
        wg=np.ascontiguousarray(np.asarray(L0_attn_Wg, np.float32)),
        wo=np.ascontiguousarray(np.asarray(L0_attn_Wo, np.float32)),
        beta=np.ascontiguousarray(np.asarray(L0_attn_beta, np.float32).reshape(1, H)),
        lnw=lnw, lnb=lnb, jrev=tabs["jrev"], eye=tabs["eye"],
        ones128=tabs["ones128"], onescol=tabs["onescol"],
        f1t=tabs["f1"], f2t=tabs["f2"], f3t=tabs["f3"], mskt=tabs["msk"],
        t5g=t5g, t5u=t5u, t5d=t5d,
    )
    zeros_pref = np.zeros((T, D), np.float32)
    in_maps = []
    for core in range(8):
        b = core // 2; hf = core % 2
        m = dict(shared)
        m["xT"] = np.ascontiguousarray(x[b, hf * T:(hf + 1) * T, :].T)
        m["xp"] = zeros_pref if hf == 0 else np.ascontiguousarray(x[b, 0:T, :])
        in_maps.append(m)
    nc = build_nc(debug=_debug)
    res = run_bass_kernel_spmd(nc, in_maps, core_ids=list(range(8)))
    out = np.empty((4, 4096, 2048), np.float32)
    for core in range(8):
        b = core // 2; hf = core % 2
        out[b, hf * T:(hf + 1) * T, :] = res.results[core]["oT"].T
    if _debug:
        kernel._dbg = res.results
    return out


# version 12
# speedup vs baseline: 1.0073x; 1.0073x over previous
import numpy as np
import ml_dtypes
import concourse.bass as bass
import concourse.mybir as mybir
from concourse.bass_utils import run_bass_kernel_spmd

F32 = mybir.dt.float32
BF16 = mybir.dt.float16
F16 = mybir.dt.float16
U8 = mybir.dt.uint8
AF = mybir.ActivationFunctionType
ALU = mybir.AluOpType
AXX = mybir.AxisListType.X

D = 2048; T = 2048; H = 16; II = 8192
KC = D // 128
NS = II * D
NB = NS // 128
NCTRL = 130156
NT = NB // 128
NB2 = NS // 256
NT2 = NB2 // 128
NSUP = NT // 8
LN_EPS = 1e-5

_HOST = {}

def _host_tables():
    if _HOST:
        return _HOST
    t = np.linspace(0.0, NCTRL - 1.0, NS, dtype=np.float64)
    ii = np.clip(np.floor(t).astype(np.int64), 0, NCTRL - 2)
    f = (t - ii).astype(np.float32)
    f2 = f * f
    f3 = f2 * f
    iblk = ii.reshape(NS // 256, 256)
    I0 = iblk[:, 0].copy()
    assert int((iblk - I0[:, None]).max()) <= 2
    mskB = (iblk == I0[:, None] + 1).astype(np.uint8).reshape(-1)
    mskC = (iblk >= I0[:, None] + 2).astype(np.uint8).reshape(-1)
    tap_idx = np.clip(I0[:, None] + np.arange(-1, 6)[None, :], 0, NCTRL - 1)
    p = np.arange(128)[:, None]; c = np.arange(KC)[None, :]
    _HOST.update(
        f1=f, f2=f2, f3=f3, msk=mskB, mskc=mskC, tap_idx=tap_idx,
        jrev=np.ascontiguousarray(((T - 1) - (128 * c + p)).astype(np.float32)),
        eye=np.eye(128, dtype=np.float16),
        ones128=np.ones((128, 128), dtype=np.float16),
        onescol=np.ones((1, 128), dtype=np.float32),
    )
    return _HOST

def _t5_planes(cp):
    tabs = _host_tables()
    t5 = np.asarray(cp, dtype=np.float32)[tabs["tap_idx"]]
    return np.ascontiguousarray(t5.reshape(NT2, 128, 7).transpose(2, 1, 0))

class _Sem:
    __slots__ = ("h", "name", "cnt")
    def __init__(self, h, name):
        self.h = h; self.name = name; self.cnt = 0

class _B:
    ENGS = ("sync", "gpsimd", "vector", "scalar", "tensor")
    def __init__(self, sems, dmasems):
        self.streams = {e: [] for e in self.ENGS}
        self.sems = sems
        self.dmasems = dmasems
        self.waited = {}
        self.lastw = {}
        self.readers = {}
    def _target(self, semname, cnt):
        if semname in self.dmasems:
            return self.sems[semname].cnt
        return cnt
    def _wait(self, eng, semname, cnt):
        if cnt <= 0:
            return
        key = (eng, semname)
        if self.waited.get(key, 0) >= cnt:
            return
        self.waited[key] = cnt
        h = self.sems[semname].h
        self.streams[eng].append(lambda e, h=h, c=cnt: e.wait_ge(h, c))
    def rd(self, eng, *bufs):
        for b in bufs:
            w = self.lastw.get(b)
            if w:
                self._wait(eng, w[0], self._target(w[0], w[1]))
    def war(self, eng, *bufs):
        for b in bufs:
            w = self.lastw.get(b)
            if w:
                self._wait(eng, w[0], self._target(w[0], w[1]))
            for sn, c in self.readers.get(b, {}).items():
                self._wait(eng, sn, self._target(sn, c))
    def op(self, eng, fn, semname, inc=1, w=(), r=()):
        self.rd(eng, *r)
        self.war(eng, *w)
        s = self.sems[semname]
        s.cnt += inc
        cnt = s.cnt
        h = s.h
        self.streams[eng].append(lambda e, fn=fn, h=h, i=inc: fn(e).then_inc(h, i))
        for b in w:
            self.lastw[b] = (semname, cnt)
            self.readers[b] = {}
        for b in r:
            self.readers.setdefault(b, {})[semname] = cnt
        return cnt
    def plain(self, eng, fn):
        self.streams[eng].append(fn)

_NC_CACHE = {}

def build_nc(debug=()):
    key = ("v1", tuple(sorted(debug)))
    if key in _NC_CACHE:
        return _NC_CACHE[key]
    nc = bass.Bass("TRN2", target_bir_lowering=False)

    def inp(name, shape, dt=F32):
        return nc.dram_tensor(name, list(shape), dt, kind="ExternalInput").ap()

    xT = inp("xT", (D, T))
    xp = inp("xp", (T, D))
    wr_ = [inp("wq", (D, D)), inp("wv", (D, D)), inp("wg", (D, D)), inp("wo", (D, D))]
    beta = inp("beta", (1, H))
    lnww = inp("lnw", (128, KC)); lnbb = inp("lnb", (128, KC))
    jrev = inp("jrev", (128, KC))
    eye = inp("eye", (128, 128), BF16)
    ones128 = inp("ones128", (128, 128), BF16)
    onescol = inp("onescol", (1, 128))
    f1t = inp("f1t", (NS,)); f2t = inp("f2t", (NS,)); f3t = inp("f3t", (NS,))
    mskt = inp("mskt", (NS,), U8)
    mskct = inp("mskct", (NS,), U8)
    t5in = [inp("t5g", (7, 128, NT2)), inp("t5u", (7, 128, NT2)),
            inp("t5d", (7, 128, NT2))]

    oT = nc.dram_tensor("oT", [D, T], F32, kind="ExternalOutput").ap()
    dbg = {}
    _dbgdt = {"d_x1": BF16, "d_y": BF16, "d_xmid": F32, "d_x2": BF16,
              "d_g": BF16, "d_state": BF16, "d_carry": F32, "d_gw": BF16,
              "d_dw": BF16, "d_inter": BF16}
    for nm, shape in (("d_x1", (D, T)), ("d_y", (D, T)), ("d_xmid", (D, T)),
                      ("d_x2", (D, T)), ("d_g", (D, T)), ("d_state", (D, T)),
                      ("d_carry", (128, H)), ("d_gw", (II, D)), ("d_dw", (D, II)),
                      ("d_inter", (II, T))):
        if nm in debug:
            dbg[nm] = nc.dram_tensor(nm, list(shape), _dbgdt[nm],
                                     kind="ExternalOutput").ap()

    def dram(name, shape, dt):
        return nc.dram_tensor(name, list(shape), dt, kind="Internal").ap()

    w_b = [dram("wq_b", (D, D), BF16), dram("wv_b", (D, D), BF16),
           dram("wg_b", (D, D), BF16), dram("wo_b", (D, D), BF16)]
    w_bT = [dram("wq_bT", (D, D), BF16), dram("wv_bT", (D, D), BF16),
            dram("wg_bT", (D, D), BF16), dram("wo_bT", (D, D), BF16)]
    g_dn = dram("g_dn", (D, II), BF16)
    g_gate = dram("g_gate", (II, D), BF16)
    g_up = dram("g_up", (II, D), BF16)
    gT_dn = dram("g_dnT", (II, D), BF16)
    gT_gate = dram("g_gateT", (D, II), BF16)
    gT_up = dram("g_upT", (D, II), BF16)
    y_dr = dram("y_dr", (D, T), BF16)
    gg_dr = dram("gg_dr", (D, T), BF16)
    xmid_dr = dram("xmid_dr", (D, T), F32)
    xmidb_dr = dram("xmidb_dr", (D, T), BF16)
    inter_dr = dram("inter_dr", (II, T), BF16)

    ctxs = []
    def sbuf(name, shape, dt):
        t = nc.sbuf_tensor(name, list(shape), dt); ctxs.append(t); return t.__enter__()
    def psum(name, shape, dt):
        t = nc.psum_tensor(name, list(shape), dt); ctxs.append(t); return t.__enter__()

    AB = sbuf("AB", [128, 65536], BF16)
    APLr = AB[:, 0:18432].rearrange("p (c q) -> p c q", q=36)
    XY = AB[:, 24576:57344]
    def xyc(c):
        return XY[:, c * 2048:(c + 1) * 2048]
    ABi = AB.rearrange("p (ic t) -> p ic t", t=1024)
    XYTOK = tuple(f"xy{c}" for c in range(KC))
    APTOK = tuple(f"apl{q}" for q in range(36))
    ABTOK = tuple(f"abg{g}" for g in range(8))

    ARENA = sbuf("ARENA", [128, 8192], BF16)
    ARTOK = ("AR0", "AR1", "AR2", "AR3")
    MRB = sbuf("MRB", [128, 4096], F16)
    FT = sbuf("FT", [128, 6144], F16)
    MKT = sbuf("MKT", [128, 2048], U8)
    GST = sbuf("GST", [128, 4096], BF16)
    WRK = sbuf("WRK", [128, 1216], F32)
    SMA = sbuf("SMA", [128, 384], F32)
    SMB = sbuf("SMB", [128, 1024], BF16)
    EVC = sbuf("EVC", [128, 6144], BF16)
    XFT = sbuf("XFT", [128, 2048], F32)
    T5Q = XFT[:, 0:1280]
    BT = sbuf("BT", [128, 512], BF16)
    BNC = sbuf("BNC", [128, 2048], BF16)

    pW = [psum(f"pW{i}", [128, 512], F32) for i in range(4)]
    pacc = [psum(f"pA{i}", [128, 512], F32) for i in range(2)]
    pQc = psum("pQc", [128, 512], F32)
    pXs = psum("pXs", [128, 512], F16)

    DMASEMS = ("sq", "sqG", "sqG1", "sIN0", "sIN1", "sWS0", "sWS1", "sWB2", "sFX", "sOT", "sBD", "sC", "sWB0", "sWB1", "sWB2c", "sWB3", "sT5", "sF0",
               "sF1", "sMK", "sLS0", "sLS1", "sXT", "sWSL", "sYG", "sFA0",
               "sFA1", "sFBI", "sFBD", "sSG", "sDBG")
    sems = {}
    for nm in DMASEMS + ("sv", "ss", "st", "sp"):
        t = nc.semaphore(nm); ctxs.append(t)
        sems[nm] = _Sem(t.__enter__(), nm)
    B = _B(sems, set(DMASEMS))

    def dma(q, sem, out, in_, w=(), r=()):
        return B.op(q, lambda e, o=out, i=in_: e.dma_start(out=o, in_=i), sem, 16, w=w, r=r)
    def dmaT(q, sem, out, in_, w=(), r=()):
        return B.op(q, lambda e, o=out, i=in_: e.dma_start_transpose(o, i), sem, 16, w=w, r=r)

    eye_s = SMB[:, 0:128]; ones_s = SMB[:, 128:256]
    st_s = SMB[:, 256:512]
    wpb_s = [SMB[:, 512 + 16 * i:512 + 16 * (i + 1)] for i in range(2)]
    mub_s = [SMB[:, 544 + 2 + i:544 + 3 + i] for i in range(2)]
    tB2 = None
    onescol_s = SMA[0:1, 0:128]
    beta_s = SMA[0:1, 128:144]
    jrev_s = SMA[:, 160:176]
    lnw_s = SMA[:, 176:192]; lnb_s = SMA[:, 192:208]
    lam_s = SMA[:, 208:224]; lnl_s = SMA[:, 224:240]
    pmu_s = SMA[:, 240:256]; pr_s = SMA[:, 256:272]
    carry_s = SMA[:, 288:304]
    wp_s = SMA[:, 304:320]
    mu1 = MRB[:, 0:2048]; r1 = MRB[:, 2048:4096]
    eps_c = SMA[:, 320:321]; mhalf_c = SMA[:, 321:322]
    tmp0 = WRK[:, 0:512]; tmp1 = WRK[:, 512:1024]
    asc_s = WRK[:, 1024:1216]
    tbB_ = [[AB[:, 18432 + (2 * m + par) * 256:18432 + (2 * m + par + 1) * 256]
             for par in range(2)] for m in range(3)]
    tbC_ = [[AB[:, 19968 + (2 * m + par) * 256:19968 + (2 * m + par + 1) * 256]
             for par in range(2)] for m in range(3)]
    qh = EVC[:, 0:2048]; vh = EVC[:, 2048:4096]; syh = EVC[:, 4096:6144]
    spbv = EVC[0:16, 2048:4096]
    wsl_ = [ARENA[:, 0:2048], ARENA[:, 2048:4096]]
    EQ = ("Eq0", "Eq1"); EV = ("Ev0", "Ev1"); ES = ("Es0", "Es1")

    B.op("vector", lambda e: e.memset(eps_c, LN_EPS), "sv", w=("epsc",))
    B.op("vector", lambda e: e.memset(mhalf_c, -0.5), "sv", w=("epsc",))
    dma("sync", "sC", eye_s, eye, w=("eye",))
    dma("sync", "sC", ones_s, ones128, w=("ones",))
    dma("sync", "sC", onescol_s, onescol, w=("onescol",))
    dma("sync", "sC", beta_s, beta, w=("beta",))
    dma("sync", "sC", jrev_s, jrev, w=("jrev",))
    dma("sync", "sC", lnw_s, lnww, w=("lnw",))
    dma("sync", "sC", lnb_s, lnbb, w=("lnb",))

    B.rd("tensor", "onescol", "beta")
    B.op("tensor", lambda e: e.matmul(pacc[0][:, 0:H], onescol_s, beta_s,
                                      start=True, stop=True), "st", w=("pacc0",))
    B.rd("scalar", "pacc0")
    B.op("scalar", lambda e: e.activation(lam_s, pacc[0][:, 0:H], AF.Sigmoid),
         "ss", w=("lam",), r=("pacc0",))
    B.rd("vector", "lam")
    B.op("vector", lambda e: e.tensor_single_scalar(lnl_s, lam_s, 1e-38, ALU.max),
         "sv", w=("lnl",))
    B.op("scalar", lambda e: e.activation(lnl_s, lnl_s, AF.Ln), "ss",
         w=("lnl",), r=("lnl",))

    ACOEF = [[(1, 1.0)],
             [(0, -0.5), (2, 0.5)],
             [(0, 1.0), (1, -2.5), (2, 2.0), (3, -0.5)],
             [(0, -0.5), (1, 1.5), (2, -1.5), (3, 0.5)]]
    NQ = NT2 // 2
    for mi in range(3):
        for qb in range(2):
            c0 = qb * NQ
            t5v = [T5Q[:, j * NQ:(j + 1) * NQ] for j in range(5)]
            t5w = [XFT[:, 1280 + jj * NQ:1280 + (jj + 1) * NQ] for jj in range(2)]
            for j in range(7):
                B.war("sync", "XF0", "XF1")
                dst = t5v[j] if j < 5 else t5w[j - 5]
                dma("sync", "sT5", dst, t5in[mi][j, :, c0:c0 + NQ],
                    w=("XF0", "XF1") if j == 6 else ())
            t5all = t5v + t5w
            for part in range(3):
                for q in range(4):
                    terms = [(tap + part, co) for tap, co in ACOEF[q]]
                    dq = mi * 12 + part * 4 + q
                    dst = APLr[:, c0:c0 + NQ, dq]
                    for j, (tap, co) in enumerate(terms):
                        last = (j == len(terms) - 1)
                        o = dst if last else (tmp0[:, 0:NQ] if j % 2 == 0 else tmp1[:, 0:NQ])
                        wtok = (f"apl{dq}",) if last else ("t0" if j % 2 == 0 else "t1",)
                        B.war("vector", *wtok)
                        if j == 0:
                            B.op("vector", lambda e, o=o, i=t5all[tap], co=co:
                                 e.tensor_single_scalar(o, i, co, ALU.mult),
                                 "sv", w=wtok, r=("XF0",))
                        else:
                            prev = tmp0[:, 0:NQ] if (j - 1) % 2 == 0 else tmp1[:, 0:NQ]
                            ptok = "t0" if (j - 1) % 2 == 0 else "t1"
                            B.rd("vector", ptok)
                            B.op("vector", lambda e, o=o, i=t5all[tap], co=co, pv=prev:
                                 e.scalar_tensor_tensor(out=o, in0=i, scalar=co, in1=pv,
                                                        op0=ALU.mult, op1=ALU.add),
                                 "sv", w=wtok, r=("XF0", ptok))

    _wsem = ("sWB0", "sWB1", "sWB2c", "sWB3")
    _cast_state = {"k": 0}
    _cast_order = [(wi, c8) for wi in (1, 0, 2, 3) for c8 in range(4)]
    def cast_pump(n):
        k = _cast_state["k"]
        for j in range(k, min(k + n, 16)):
            wi, c8 = _cast_order[j]
            dma("gpsimd", _wsem[wi], w_b[wi][512 * c8:512 * (c8 + 1), :],
                wr_[wi][512 * c8:512 * (c8 + 1), :],
                w=(f"wb{wi}",) if c8 == 3 else ())
        _cast_state["k"] = min(k + n, 16)

    def build_wT(wi, c0, nc_):
        B.rd("scalar", f"wb{wi}")
        for c in range(c0, c0 + nc_):
            for hf in range(2):
                buf = BNC[:, hf * 1024:hf * 1024 + 1024]
                B.war("scalar", f"bnc{hf}")
                B.op("scalar", lambda e, o=buf,
                     i=w_b[wi][1024 * hf:1024 * (hf + 1), c * 128:(c + 1) * 128]:
                     e.dma_start_transpose(o, i), "sWB2", 16, w=(f"bnc{hf}",))
                B.rd("scalar", f"bnc{hf}")
                B.op("scalar", lambda e,
                     o=w_bT[wi][c * 128:(c + 1) * 128, 1024 * hf:1024 * (hf + 1)],
                     i=buf:
                     e.dma_start(out=o, in_=i), "sWB2", 16,
                     r=(f"bnc{hf}",), w=(f"wT{wi}",))

    def flat4(t):
        return t.rearrange("(s tj p j) -> s p tj j", tj=4, p=128, j=256)
    f1f = flat4(f1t); f2f = flat4(f2t); f3f = flat4(f3t); mkf = flat4(mskt)
    mkcf = flat4(mskct)
    gmats = (g_dn, g_gate, g_up)
    gmi = (2, 0, 1)
    gfl = [flat4(g.rearrange("a b -> (a b)")) for g in gmats]
    gw_cnt = [dict() for _ in range(3)]

    def blend_supertile(s):
        bi = s % 2
        f1b = FT[:, bi * 1024:bi * 1024 + 1024]
        f2b = FT[:, 2048 + bi * 1024:2048 + bi * 1024 + 1024]
        f3b = FT[:, 4096 + bi * 1024:4096 + bi * 1024 + 1024]
        mkb = MKT[:, 0:1024]
        mkc = MKT[:, 1024:2048]
        fv = lambda b, tj: b.rearrange("p (tj j) -> p tj j", j=256)[:, tj, :]
        B.war("gpsimd", f"fb1{bi}", f"fb2{bi}", f"fb3{bi}")
        dma("gpsimd", "sF0" if bi == 0 else "sF1",
            f1b.rearrange("p (tj j) -> p tj j", j=256), f1f[s], w=(f"fb1{bi}",))
        dma("gpsimd", "sF0" if bi == 0 else "sF1",
            f2b.rearrange("p (tj j) -> p tj j", j=256), f2f[s], w=(f"fb2{bi}",))
        dma("gpsimd", "sF0" if bi == 0 else "sF1",
            f3b.rearrange("p (tj j) -> p tj j", j=256), f3f[s], w=(f"fb3{bi}",))
        B.war("gpsimd", "mk")
        dma("gpsimd", "sMK", mkb.rearrange("p (tj j) -> p tj j", j=256),
            mkf[s], w=("mk",))
        dma("gpsimd", "sMK", mkc.rearrange("p (tj j) -> p tj j", j=256),
            mkcf[s], w=("mk",))
        for m in range(3):
            mi = gmi[m]
            asc = asc_s[:, m * 64:(m + 1) * 64][:, 0:48]
            srcp = APLr[:, 4 * s:4 * s + 4, mi * 12:mi * 12 + 12]
            B.op("scalar", lambda e, a=asc, sr=srcp: e.copy(out=a, in_=sr),
                 "ss", w=(f"asc{m}",),
                 r=tuple(f"apl{mi * 12 + q}" for q in range(12)))

        def emit_s(tj):
            fja = fv(f1b, tj)
            for m in range(3):
                asc = asc_s[:, m * 64:(m + 1) * 64]
                a = [asc[:, tj * 12 + q:tj * 12 + q + 1] for q in range(12)]
                slot = (bi if m == 0 else (2 + bi if m == 1 else 1 - bi))
                gs = GST[:, slot * 1024 + tj * 256:slot * 1024 + tj * 256 + 256]
                gtok = f"gs{slot}_{tj}"
                fr1 = (f"fb1{bi}", f"asc{m}")
                par = tj % 2
                tbB = tbB_[m][par]
                tbC = tbC_[m][par]
                B.op("scalar", lambda e, o=gs, i=fja, s1=a[1], s2=a[0]:
                     e.activation(o, i, AF.Identity, scale=s1, bias=s2),
                     "ss", w=(gtok,), r=fr1)
                B.op("scalar", lambda e, o=tbB, i=fja, s1=a[5], s2=a[4]:
                     e.activation(o, i, AF.Identity, scale=s1, bias=s2),
                     "ss", w=(f"tbB{m}{par}",), r=fr1)
                B.op("scalar", lambda e, o=tbC, i=fja, s1=a[9], s2=a[8]:
                     e.activation(o, i, AF.Identity, scale=s1, bias=s2),
                     "ss", w=(f"tbC{m}{par}",), r=fr1)

        def emit_v(tj):
            fjb = fv(f2b, tj); fjc = fv(f3b, tj)
            mkj = fv(mkb, tj); mkjc = fv(mkc, tj)
            for m in range(3):
                asc = asc_s[:, m * 64:(m + 1) * 64]
                a = [asc[:, tj * 12 + q:tj * 12 + q + 1] for q in range(12)]
                slot = (bi if m == 0 else (2 + bi if m == 1 else 1 - bi))
                gs = GST[:, slot * 1024 + tj * 256:slot * 1024 + tj * 256 + 256]
                gtok = f"gs{slot}_{tj}"
                fr2 = (f"fb2{bi}", f"asc{m}")
                fr3 = (f"fb3{bi}", f"asc{m}")
                par = tj % 2
                tbB = tbB_[m][par]; tBt = f"tbB{m}{par}"
                tbC = tbC_[m][par]; tCt = f"tbC{m}{par}"
                B.op("vector", lambda e, o=gs, i=fjb, s1=a[2]:
                     e.scalar_tensor_tensor(out=o, in0=i, scalar=s1, in1=o,
                                            op0=ALU.mult, op1=ALU.add), "sv",
                     w=(gtok,), r=fr2 + (gtok,))
                B.op("vector", lambda e, o=gs, i=fjc, s1=a[3]:
                     e.scalar_tensor_tensor(out=o, in0=i, scalar=s1, in1=o,
                                            op0=ALU.mult, op1=ALU.add), "sv",
                     w=(gtok,), r=(gtok,) + fr3)
                B.op("vector", lambda e, o=tbB, i=fjb, s1=a[6]:
                     e.scalar_tensor_tensor(out=o, in0=i, scalar=s1, in1=o,
                                            op0=ALU.mult, op1=ALU.add), "sv",
                     w=(tBt,), r=fr2 + (tBt,))
                B.op("vector", lambda e, o=tbB, i=fjc, s1=a[7]:
                     e.scalar_tensor_tensor(out=o, in0=i, scalar=s1, in1=o,
                                            op0=ALU.mult, op1=ALU.add), "sv",
                     w=(tBt,), r=fr3 + (tBt,))
                B.op("vector", lambda e, o=tbC, i=fjb, s1=a[10]:
                     e.scalar_tensor_tensor(out=o, in0=i, scalar=s1, in1=o,
                                            op0=ALU.mult, op1=ALU.add), "sv",
                     w=(tCt,), r=fr2 + (tCt,))
                B.op("vector", lambda e, o=tbC, i=fjc, s1=a[11]:
                     e.scalar_tensor_tensor(out=o, in0=i, scalar=s1, in1=o,
                                            op0=ALU.mult, op1=ALU.add), "sv",
                     w=(tCt,), r=fr3 + (tCt,))
                B.op("vector", lambda e, o=gs, mk=mkj, d=tbB:
                     e.copy_predicated(out=o, mask=mk, data=d), "sv",
                     w=(gtok,), r=(tBt, "mk"))
                B.op("vector", lambda e, o=gs, mk=mkjc, d=tbC:
                     e.copy_predicated(out=o, mask=mk, data=d), "sv",
                     w=(gtok,), r=(tCt, "mk"))

        emit_s(0)
        for tj in range(1, 4):
            emit_s(tj)
            emit_v(tj - 1)
        emit_v(3)
        for m in range(3):
            slot = (bi if m == 0 else (2 + bi if m == 1 else 1 - bi))
            gtoks = tuple(f"gs{slot}_{tj}" for tj in range(4))
            B.rd("sync", *gtoks)
            gs = GST[:, slot * 1024:slot * 1024 + 1024]
            c = dma("sync", "sqG" if s % 2 == 0 else "sqG1", gfl[m][s],
                    gs.rearrange("p (tj j) -> p tj j", j=256),
                    r=gtoks, w=(f"G{m}",) + gtoks)
            gw_cnt[m][s] = c

    def bounce_gu_kilo(k):
        for m, gsrc, gdst in ((1, g_gate, gT_gate), (2, g_up, gT_up)):
            assert (16 * k + 15) in gw_cnt[m]
            B._wait("sync", "sqG", sems["sqG"].cnt)
            B._wait("sync", "sqG1", sems["sqG1"].cnt)
            for p0 in range(16):
                bi = p0 % 2
                buf = BNC[:, bi * 1024:bi * 1024 + 1024]
                B.war("sync", f"bnc{bi}")
                dmaT("sync", "sqG", buf,
                     gsrc[1024 * k:1024 * (k + 1), 128 * p0:128 * (p0 + 1)],
                     w=(f"bnc{bi}",))
                B.rd("sync", f"bnc{bi}")
                dma("sync", "sqG",
                    gdst[128 * p0:128 * (p0 + 1), 1024 * k:1024 * (k + 1)],
                    buf, r=(f"bnc{bi}",), w=(f"gT{m}k{k}",))

    def bounce_dn_pairs(p0, n):
        if p0 == 0:
            assert (NSUP - 1) in gw_cnt[0]
            B._wait("scalar", "sqG", sems["sqG"].cnt)
            B._wait("scalar", "sqG1", sems["sqG1"].cnt)
        for p in range(p0, p0 + n):
            ic = p // 2; hf = p % 2
            buf = BNC[:, hf * 1024:hf * 1024 + 1024]
            B.war("scalar", f"bnc{hf}")
            B.op("scalar", lambda e, o=buf, i=g_dn[1024 * hf:1024 * (hf + 1),
                                                   128 * ic:128 * (ic + 1)]:
                 e.dma_start_transpose(o, i), "sBD", 16, w=(f"bnc{hf}",))
            B.rd("scalar", f"bnc{hf}")
            B.op("scalar", lambda e, o=gT_dn[128 * ic:128 * (ic + 1),
                                             1024 * hf:1024 * (hf + 1)], i=buf:
                 e.dma_start(out=o, in_=i), "sBD", 16, r=(f"bnc{hf}",), w=("gT0",))

    _pump = {"next": 0}
    def pump(n):
        s0 = _pump["next"]
        for s in range(s0, min(s0 + n, NSUP)):
            blend_supertile(s)
            if s % 16 == 15:
                bounce_gu_kilo(s // 16)
        _pump["next"] = min(s0 + n, NSUP)

    def ln_stats(src_dr, tag, bf16src):
        for half in range(2):
            for c in range(KC):
                sl = c % 2
                src = src_dr[c * 128:(c + 1) * 128, half * 1024:(half + 1) * 1024]
                xb = EVC[:, sl * 1024:sl * 1024 + 1024]
                if bf16src:
                    B.war("sync", EQ[sl])
                    dma("sync", "sLS0" if sl == 0 else "sLS1", xb, src, w=(EQ[sl],))
                else:
                    xf = XFT[:, sl * 1024:sl * 1024 + 1024]
                    B.war("sync", f"XF{sl}")
                    dma("sync", "sLS0" if sl == 0 else "sLS1", xf, src, w=(f"XF{sl}",))
                    B.rd("gpsimd", f"XF{sl}")
                    B.war("gpsimd", EQ[sl])
                    B.op("gpsimd", lambda e, o=xb, i=xf: e.tensor_copy(out=o, in_=i),
                         "sp", w=(EQ[sl],), r=(f"XF{sl}",))
                xq = EVC[:, 2048 + sl * 1024:2048 + sl * 1024 + 1024]
                B.rd("scalar", EQ[sl])
                B.war("scalar", EV[sl])
                B.op("scalar", lambda e, o=xq, i=xb: e.activation(o, i, AF.Square),
                     "ss", w=(EV[sl],), r=(EQ[sl],))
                st0 = (c == 0); sp0 = (c == KC - 1)
                B.rd("tensor", "ones", EQ[sl], EV[sl])
                if st0:
                    B.war("tensor", "pW0", "pW1", "pW2", "pW3")
                for j in range(2):
                    fn1 = lambda e, p=pW[j], i=xb[:, j * 512:(j + 1) * 512], s0=st0, s1=sp0: \
                        e.matmul(p[:], ones_s, i, start=s0, stop=s1)
                    fn2 = lambda e, p=pW[2 + j], i=xq[:, j * 512:(j + 1) * 512], s0=st0, s1=sp0: \
                        e.matmul(p[:], ones_s, i, start=s0, stop=s1)
                    if sp0:
                        B.op("tensor", fn1, "st", w=(f"pW{j}",), r=(EQ[sl],))
                        B.op("tensor", fn2, "st", w=(f"pW{2 + j}",), r=(EV[sl],))
                    elif j == 1:
                        B.plain("tensor", lambda e, f=fn1: f(e))
                        B.op("tensor", fn2, "st", r=(EQ[sl], EV[sl]))
                    else:
                        B.plain("tensor", lambda e, f=fn1: f(e))
                        B.plain("tensor", lambda e, f=fn2: f(e))
            for j in range(2):
                tok0 = half * 1024 + j * 512
                mu_sl = mu1[:, tok0:tok0 + 512]; r_sl = r1[:, tok0:tok0 + 512]
                B.rd("vector", f"pW{j}", f"pW{2 + j}")
                B.war("vector", "t0", "t1", "mu_c")
                B.op("vector", lambda e, p=pW[j]:
                     e.tensor_single_scalar(tmp0, p[:], 1.0 / D, ALU.mult),
                     "sv", w=("t0",), r=(f"pW{j}",))
                B.op("vector", lambda e, m=mu_sl: e.tensor_copy(out=m, in_=tmp0),
                     "sv", w=("mu_c",), r=("t0",))
                B.op("vector", lambda e: e.tensor_mul(tmp1, tmp0, tmp0),
                     "sv", w=("t1",), r=("t0",))
                B.op("vector", lambda e, p=pW[2 + j]:
                     e.scalar_tensor_tensor(out=tmp0, in0=p[:], scalar=1.0 / D,
                                            in1=tmp1, op0=ALU.mult, op1=ALU.subtract),
                     "sv", w=("t0",), r=(f"pW{2 + j}", "t1"))
                B.rd("scalar", "t0", "epsc")
                B.war("scalar", "r_c")
                B.op("scalar", lambda e: e.activation(tmp0, tmp0, AF.Ln, bias=eps_c),
                     "ss", w=("t0",), r=("t0", "epsc"))
                B.op("scalar", lambda e, r=r_sl: e.activation(r, tmp0, AF.Exp,
                                                              scale=mhalf_c),
                     "ss", w=("r_c",), r=("t0",))

    for k in range(8):
        pump(1)
        cast_pump(2)
    for wi in (1, 0):
        for c0 in (0, 8):
            pump(1)
            build_wT(wi, c0, 8)
    ln_stats(xT, "l1", False)
    for wi in (2, 3):
        for c0 in (0, 8):
            pump(1)
            build_wT(wi, c0, 8)

    for c in range(KC):
        sl = c % 2
        xfull = XFT[:, 0:2048]
        B.war("sync", "XF0", "XF1")
        dma("sync", "sXT", xfull, xp[c * 128:(c + 1) * 128, :], w=("XF0", "XF1"))
        xb = EVC[:, 0:2048]
        B.rd("gpsimd", "XF0")
        B.war("gpsimd", *EQ)
        B.op("gpsimd", lambda e, o=xb, i=xfull: e.tensor_copy(out=o, in_=i),
             "sp", w=EQ, r=("XF0", "XF1"))
        xq = EVC[:, 2048:4096]
        B.rd("scalar", "XF0")
        B.war("scalar", *EV)
        B.op("scalar", lambda e, o=xq, i=xfull: e.activation(o, i, AF.Square),
             "ss", w=EV, r=("XF0", "XF1"))
        B.rd("vector", "XF0", *EV)
        B.war("vector", "pst0", "pst1", "t0")
        B.op("vector", lambda e, o=pmu_s[:, c:c + 1], i=xfull:
             e.reduce_sum(o, i, axis=AXX), "sv", w=("pst0",), r=("XF0", "XF1"))
        B.op("vector", lambda e, o=pr_s[:, c:c + 1], i=xq:
             e.reduce_sum(o, i, axis=AXX), "sv", w=("pst1",), r=EV)
        B.op("vector", lambda e, o=pmu_s[:, c:c + 1]:
             e.tensor_single_scalar(o, o, 1.0 / D, ALU.mult), "sv", w=("pst0",),
             r=("pst0",))
        B.op("vector", lambda e, o=tmp0[:, 0:1], m=pmu_s[:, c:c + 1]:
             e.tensor_mul(o, m, m), "sv", w=("t0",), r=("pst0",))
        B.op("vector", lambda e, o=pr_s[:, c:c + 1], t=tmp0[:, 0:1]:
             e.scalar_tensor_tensor(out=o, in0=o, scalar=1.0 / D, in1=t,
                                    op0=ALU.mult, op1=ALU.subtract), "sv",
             w=("pst1",), r=("t0", "pst1"))
        B.rd("scalar", "pst1", "epsc")
        B.op("scalar", lambda e, o=pr_s[:, c:c + 1]:
             e.activation(o, o, AF.Ln, bias=eps_c), "ss", w=("pst1",),
             r=("pst1", "epsc"))
        B.op("scalar", lambda e, o=pr_s[:, c:c + 1]:
             e.activation(o, o, AF.Exp, scale=mhalf_c), "ss", w=("pst1",),
             r=("pst1",))
        B.rd("scalar", "lnl", "jrev")
        B.war("scalar", "wp")
        B.op("scalar", lambda e, c_=c: e.activation(wp_s, lnl_s, AF.Exp,
                                                    scale=jrev_s[:, c_:c_ + 1]),
             "ss", w=("wp",))
        B.rd("vector", "wp", "pst1", "pst0")
        B.war("vector", f"wpb{sl}", f"mub{sl}")
        B.op("vector", lambda e, o=wpb_s[sl], c_=c:
             e.tensor_scalar(out=o, in0=wp_s, scalar1=pr_s[:, c_:c_ + 1],
                             scalar2=None, op0=ALU.mult),
             "sv", w=(f"wpb{sl}",), r=("wp", "pst1"))
        B.op("vector", lambda e, o=mub_s[sl], c_=c:
             e.tensor_copy(out=o, in_=pmu_s[:, c_:c_ + 1]),
             "sv", w=(f"mub{sl}",), r=("pst0",))
        st0 = (c == 0); sp0 = (c == KC - 1)
        B.rd("tensor", f"wpb{sl}", f"mub{sl}", *EQ)
        if st0:
            B.war("tensor", "pW0", "pW1", "pW2", "pW3", "pacc0")
        for n in range(4):
            fn = lambda e, p=pW[n], l=wpb_s[sl], i=xb[:, n * 512:(n + 1) * 512], \
                s0=st0, s1=sp0: e.matmul(p[0:16, :], l, i, start=s0, stop=s1)
            if sp0:
                B.op("tensor", fn, "st", w=(f"pW{n}",), r=EQ + (f"wpb{sl}",))
            else:
                B.plain("tensor", lambda e, f=fn: f(e))
        fnm = lambda e, l=wpb_s[sl], i=mub_s[sl], s0=st0, s1=sp0: \
            e.matmul(pacc[0][0:16, 1:2], l, i, start=s0, stop=s1)
        if sp0:
            B.op("tensor", fnm, "st", w=("pacc0",), r=(f"mub{sl}", f"wpb{sl}"))
        else:
            B.op("tensor", fnm, "st", r=(EQ[0], f"wpb{sl}", f"mub{sl}"))
        if c % 4 == 3:
            pump(1)
    B.rd("vector", "pacc0")
    B.war("vector", "t0")
    B.op("vector", lambda e: e.tensor_copy(out=tmp0[0:16, 0:1], in_=pacc[0][0:16, 1:2]),
         "sv", w=("t0",), r=("pacc0",))
    B.war("vector", *EV)
    for n in range(4):
        B.rd("vector", f"pW{n}", "t0")
        B.op("vector", lambda e, n=n, p=pW[n]:
             e.tensor_scalar(out=spbv[:, n * 512:(n + 1) * 512], in0=p[0:16, :],
                             scalar1=tmp0[0:16, 0:1], scalar2=None, op0=ALU.subtract),
             "sv", w=(EV[n // 2],), r=(f"pW{n}", "t0"))
    for c in range(KC):
        B.op("tensor", lambda e, c_=c:
             e.transpose(pXs[:, 0:16], spbv[:, c_ * 128:(c_ + 1) * 128],
                         eye_s[0:16, 0:16]),
             "st", w=("pXs",), r=EV)
        B.op("scalar", lambda e, c_=c:
             e.copy(out=st_s[:, c_ * 16:(c_ + 1) * 16], in_=pXs[:, 0:16]),
             "ss", w=("st_s",), r=("pXs",))

    for c in range(KC):
        xfull = XFT[:, 0:2048]
        B.war("sync", "XF0", "XF1")
        dma("sync", "sXT", xfull, xT[c * 128:(c + 1) * 128, :], w=("XF0", "XF1"))
        x1c = xyc(c)
        B.war("vector", f"xy{c}")
        for n in range(4):
            xtok = "XF0" if n < 2 else "XF1"
            B.rd("vector", xtok, "mu_c", "r_c")
            B.war("vector", "t0")
            B.op("vector", lambda e, i=xfull[:, n * 512:(n + 1) * 512],
                 m=mu1[:, n * 512:(n + 1) * 512]:
                 e.tensor_sub(tmp0, i, m), "sv", w=("t0",), r=(xtok, "mu_c"))
            B.op("vector", lambda e, o=x1c[:, n * 512:(n + 1) * 512],
                 r_=r1[:, n * 512:(n + 1) * 512]:
                 e.tensor_mul(o, tmp0, r_), "sv", w=(f"xy{c}",), r=("t0", "r_c"))
        if c % 2 == 1:
            pump(1)
    if "d_x1" in dbg:
        for c in range(KC):
            B.rd("sync", f"xy{c}")
            dma("sync", "sq", dbg["d_x1"][c * 128:(c + 1) * 128, :], xyc(c),
                r=(f"xy{c}",))

    def w_sweep(widx, head, carry, banks, btoks):
        par = head % 2
        wsl = wsl_[par]
        wslv = wsl.rearrange("p (c a) -> p c a", c=KC)
        B.rd("sync", f"wT{widx}")
        B.war("sync", f"AR{par}")
        src_ap = w_bT[widx].rearrange("(c p) a -> p c a", p=128)[
            :, :, head * 128:(head + 1) * 128]
        dma("sync", "sWS0" if par == 0 else "sWS1", wslv, src_ap,
            w=(f"AR{par}",))
        B.rd("tensor", f"AR{par}", *XYTOK)
        B.war("tensor", *btoks)
        if carry:
            B.war("tensor", "pQc")
            B.rd("tensor", "st_s")
        for c in range(KC):
            st0 = (c == 0); sp0 = (c == KC - 1)
            for n in range(4):
                fn = (lambda e, p=banks[n], wt=wslv[:, c, :],
                      i=xyc(c)[:, n * 512:(n + 1) * 512],
                      s0=st0, s1=sp0: e.matmul(p[:], wt, i, start=s0, stop=s1))
                if sp0:
                    B.op("tensor", fn, "st", w=(btoks[n],), r=(f"AR{par}",))
                elif n == 3:
                    B.op("tensor", fn, "st", r=(f"AR{par}", f"xy{c}"))
                else:
                    B.plain("tensor", lambda e, f=fn: f(e))
            if carry:
                fnc = (lambda e, wt=wslv[:, c, :], c_=c, h_=head, s0=st0, s1=sp0:
                       e.matmul(pQc[:, 0:1], wt,
                                st_s[:, c_ * 16 + h_:c_ * 16 + h_ + 1],
                                start=s0, stop=s1))
                if sp0:
                    B.op("tensor", fnc, "st", w=("pQc",), r=("st_s", f"AR{par}"))
                else:
                    B.op("tensor", fnc, "st", r=("st_s", f"AR{par}"))

    BK_V = (pW[0], pW[1], pW[2], pW[3])
    BT_V = ("pW0", "pW1", "pW2", "pW3")
    BK_Q = (pacc[0], pacc[1], pW[2], pW[3])
    BT_Q = ("pacc0", "pacc1", "pW2", "pW3")
    for head in range(H):
        w_sweep(1, head, True, BK_V, BT_V)
        pump(3)
        for n in range(4):
            B.op("scalar", lambda e, n=n, p=BK_V[n]:
                 e.copy(out=vh[:, n * 512:(n + 1) * 512], in_=p[:]),
                 "ss", w=(EV[n // 2],), r=(BT_V[n],))
        B.op("scalar", lambda e, h_=head:
             e.copy(out=carry_s[:, h_:h_ + 1], in_=pQc[:, 0:1]),
             "ss", w=("carry",), r=("pQc",))
        lam_bc = lam_s[:, head:head + 1].broadcast_to([128, 2048])
        B.op("vector", lambda e, h_=head, lb=lam_bc:
             e.tensor_tensor_scan(out=syh, data0=lb, data1=vh,
                                  initial=carry_s[:, h_:h_ + 1],
                                  op0=ALU.mult, op1=ALU.add),
             "sv", w=ES, r=EV + ES + ("carry", "lam"))
        w_sweep(0, head, False, BK_Q, BT_Q)
        pump(3)
        for n in range(4):
            B.op("scalar", lambda e, n=n, p=BK_Q[n]:
                 e.copy(out=qh[:, n * 512:(n + 1) * 512], in_=p[:]),
                 "ss", w=(EQ[n // 2],), r=(BT_Q[n],))
        if "d_state" in dbg:
            B.rd("sync", *ES)
            dma("sync", "sq", dbg["d_state"][head * 128:(head + 1) * 128, :],
                syh, r=ES)
        B.op("vector", lambda e: e.tensor_mul(syh, qh, syh), "sv",
             w=ES, r=EQ + ES)
        B.rd("sync", *ES)
        dma("sync", "sq", y_dr[head * 128:(head + 1) * 128, :], syh, r=ES,
            w=("y_dr",))
    if "d_carry" in dbg:
        B.rd("sync", "carry")
        dma("sync", "sq", dbg["d_carry"], carry_s, r=("carry",))

    BK_O = (pacc[0], pacc[1], pQc, pW[3])
    BT_O = ("pacc0", "pacc1", "pQc", "pW3")
    for head in range(H):
        bs, bt = (BK_V, BT_V) if head % 2 == 0 else (BK_O, BT_O)
        w_sweep(2, head, False, bs, bt)
        pump(1)
        for n in range(4):
            B.op("scalar", lambda e, n=n, p=bs[n]:
                 e.copy(out=qh[:, n * 512:(n + 1) * 512], in_=p[:]),
                 "ss", w=(EQ[n // 2],), r=(bt[n],))
        B.rd("sync", *EQ)
        dma("sync", "sq", gg_dr[head * 128:(head + 1) * 128, :], qh, r=EQ,
            w=("gg_dr",))
    if "d_g" in dbg:
        B.rd("sync", "gg_dr")
        for c in range(KC):
            dma("sync", "sq", dbg["d_g"][c * 128:(c + 1) * 128, :],
                gg_dr[c * 128:(c + 1) * 128, :])
    if "d_y" in dbg:
        B.rd("sync", "y_dr")
        for c in range(KC):
            dma("sync", "sq", dbg["d_y"][c * 128:(c + 1) * 128, :],
                y_dr[c * 128:(c + 1) * 128, :])

    B.rd("sync", "y_dr")
    ln_stats(y_dr, "ly", True)
    pump(4)

    for c in range(KC):
        B.war("sync", *EQ)
        B.rd("sync", "y_dr")
        dma("sync", "sYG", qh, y_dr[c * 128:(c + 1) * 128, :], w=EQ)
        B.war("sync", *EV)
        B.rd("sync", "gg_dr")
        dma("sync", "sYG", vh, gg_dr[c * 128:(c + 1) * 128, :], w=EV)
        B.rd("scalar", *EV)
        B.war("scalar", *ES)
        B.op("scalar", lambda e: e.activation(syh, vh, AF.Sigmoid), "ss",
             w=ES, r=EV)
        y2c = xyc(c)
        B.war("vector", f"xy{c}")
        for n in range(4):
            sl_ = slice(n * 512, (n + 1) * 512)
            B.rd("vector", *EQ)
            B.rd("vector", *ES)
            B.rd("vector", "mu_c", "r_c", "lnw", "lnb")
            B.war("vector", "t0")
            B.op("vector", lambda e, i=qh[:, sl_], m=mu1[:, sl_]:
                 e.tensor_sub(tmp0, i, m), "sv", w=("t0",), r=EQ + ("mu_c",))
            B.op("vector", lambda e, r_=r1[:, sl_]:
                 e.tensor_mul(tmp0, tmp0, r_), "sv", w=("t0",), r=("r_c", "t0"))
            B.op("vector", lambda e, c_=c:
                 e.tensor_scalar(out=tmp0, in0=tmp0, scalar1=lnw_s[:, c_:c_ + 1],
                                 scalar2=lnb_s[:, c_:c_ + 1], op0=ALU.mult,
                                 op1=ALU.add), "sv", w=("t0",), r=("t0", "lnw", "lnb"))
            B.op("vector", lambda e, s=syh[:, sl_]:
                 e.tensor_mul(tmp0, tmp0, s), "sv", w=("t0",), r=ES + ("t0",))
            B.op("vector", lambda e, o=y2c[:, sl_], g=vh[:, sl_]:
                 e.tensor_mul(o, tmp0, g), "sv", w=(f"xy{c}",), r=("t0",) + EV)
        pump(1)

    for head in range(H):
        bs, bt = (BK_V, BT_V) if head % 2 == 0 else (BK_O, BT_O)
        w_sweep(3, head, False, bs, bt)
        pump(1)
        xfull = XFT[:, 0:2048]
        B.war("sync", "XF0", "XF1")
        dma("sync", "sXT", xfull, xT[head * 128:(head + 1) * 128, :], w=("XF0", "XF1"))
        for n in range(4):
            xtok = "XF0" if n < 2 else "XF1"
            B.rd("vector", bt[n], xtok)
            B.op("vector", lambda e, p=bs[n], xs=xfull[:, n * 512:(n + 1) * 512]:
                 e.tensor_add(xs, p[:], xs),
                 "sv", w=("XF0", "XF1") if n == 3 else (xtok,), r=(bt[n],))
        B.rd("sync", "XF0", "XF1")
        dma("sync", "sq", xmid_dr[head * 128:(head + 1) * 128, :], xfull,
            r=("XF0", "XF1"), w=("xmid",))
        B.rd("gpsimd", "XF0", "XF1")
        dma("gpsimd", "sSG", xmidb_dr[head * 128:(head + 1) * 128, :], xfull,
            r=("XF0", "XF1"), w=("xmidb",))
        pump(1)
    if "d_xmid" in dbg:
        B.rd("sync", "xmid")
        for c in range(KC):
            dma("sync", "sq", dbg["d_xmid"][c * 128:(c + 1) * 128, :],
                xmid_dr[c * 128:(c + 1) * 128, :])

    B.rd("sync", "xmidb")
    ln_stats(xmidb_dr, "l2", True)
    pump(4)
    for c in range(KC):
        B.war("sync", *EQ)
        dma("sync", "sYG", qh, xmidb_dr[c * 128:(c + 1) * 128, :], w=EQ)
        x2c = xyc(c)
        B.war("vector", f"xy{c}")
        for n in range(4):
            sl_ = slice(n * 512, (n + 1) * 512)
            B.rd("vector", *EQ)
            B.rd("vector", "mu_c", "r_c")
            B.war("vector", "t0")
            B.op("vector", lambda e, i=qh[:, sl_], m=mu1[:, sl_]:
                 e.tensor_sub(tmp0, i, m), "sv", w=("t0",), r=EQ + ("mu_c",))
            B.op("vector", lambda e, o=x2c[:, sl_], r_=r1[:, sl_]:
                 e.tensor_mul(o, tmp0, r_), "sv", w=(f"xy{c}",), r=("t0", "r_c"))
        pump(1)
    if "d_x2" in dbg:
        for c in range(KC):
            B.rd("sync", f"xy{c}")
            dma("sync", "sq", dbg["d_x2"][c * 128:(c + 1) * 128, :], xyc(c),
                r=(f"xy{c}",))
    pump(NSUP)

    gTgr = gT_gate.rearrange("(a s) i -> s a i", s=128)
    gTur = gT_up.rearrange("(a s) i -> s a i", s=128)
    sgv_ = [EVC[:, 0:512], EVC[:, 2048:2560]]
    gcv_ = [EVC[:, 512:1024], EVC[:, 2560:3072]]
    ttv_ = [EVC[:, 1024:1536], EVC[:, 3072:3584]]
    itv_ = [EVC[:, 1536:2048], EVC[:, 3584:4096]]
    for rng in range(64):
        par = rng % 2
        gsl = ARENA[:, par * 2048:par * 2048 + 2048]
        usl = ARENA[:, 4096 + par * 2048:4096 + par * 2048 + 2048]
        gslv = gsl.rearrange("p (a i) -> p a i", a=16)
        uslv = usl.rearrange("p (a i) -> p a i", a=16)
        gt = ARTOK[par]; ut = ARTOK[2 + par]
        B.war("sync", gt)
        B.rd("sync", f"gT1k{rng // 8}")
        dma("sync", "sFA0" if par == 0 else "sFA1", gslv, gTgr[:, :, rng * 128:(rng + 1) * 128], w=(gt,))
        B.war("sync", ut)
        B.rd("sync", f"gT2k{rng // 8}")
        dma("sync", "sFA0" if par == 0 else "sFA1", uslv, gTur[:, :, rng * 128:(rng + 1) * 128], w=(ut,))
        for n in range(4):
            pgi = (2 * n) % 4; pui = (2 * n + 1) % 4
            B.rd("tensor", gt, ut, *XYTOK)
            B.war("tensor", f"pW{pgi}", f"pW{pui}")
            for p0 in range(KC):
                st0 = (p0 == 0); sp0 = (p0 == KC - 1)
                fn1 = lambda e, p=pW[pgi], l=gslv[:, p0, :], \
                    i=xyc(p0)[:, n * 512:(n + 1) * 512], s0=st0, s1=sp0: \
                    e.matmul(p[:], l, i, start=s0, stop=s1)
                fn2 = lambda e, p=pW[pui], l=uslv[:, p0, :], \
                    i=xyc(p0)[:, n * 512:(n + 1) * 512], s0=st0, s1=sp0: \
                    e.matmul(p[:], l, i, start=s0, stop=s1)
                if sp0:
                    B.op("tensor", fn1, "st", w=(f"pW{pgi}",),
                         r=(gt,) + XYTOK)
                    B.op("tensor", fn2, "st", w=(f"pW{pui}",), r=(ut,))
                else:
                    B.plain("tensor", lambda e, f=fn1: f(e))
                    B.plain("tensor", lambda e, f=fn2: f(e))
            np_ = n % 2
            sgv = sgv_[np_]; gcv = gcv_[np_]; ttv = ttv_[np_]; itv = itv_[np_]
            et0 = EQ[0] if np_ == 0 else EV[0]
            et1 = EQ[1] if np_ == 0 else EV[1]
            B.op("scalar", lambda e, p=pW[pgi], o=sgv: e.activation(o, p[:], AF.Sigmoid),
                 "ss", w=(et0,), r=(f"pW{pgi}",))
            B.op("scalar", lambda e, p=pW[pgi], o=gcv: e.copy(out=o, in_=p[:]),
                 "ss", w=(et0,), r=(f"pW{pgi}",))
            B.op("vector", lambda e, o=ttv, a_=sgv, b_=gcv: e.tensor_mul(o, a_, b_),
                 "sv", w=(et1,), r=(et0,))
            B.op("vector", lambda e, p=pW[pui], o=itv, t=ttv: e.tensor_mul(o, t, p[:]),
                 "sv", w=(et1,), r=(f"pW{pui}", et1))
            dma("gpsimd", "sIN0" if np_ == 0 else "sIN1",
                inter_dr[rng * 128:(rng + 1) * 128,
                         n * 512:(n + 1) * 512], itv,
                r=(et1,), w=("inter",))
        if rng >= 32:
            bounce_dn_pairs((rng - 32) * 4, 4)
    if "d_inter" in dbg:
        B.rd("sync", "inter")
        for c in range(II // 128):
            dma("sync", "sq", dbg["d_inter"][c * 128:(c + 1) * 128, :],
                inter_dr[c * 128:(c + 1) * 128, :])

    if "d_gw" in dbg:
        B.rd("sync", "G1")
        for c in range(II // 128):
            dma("sync", "sq", dbg["d_gw"][c * 128:(c + 1) * 128, :],
                g_gate[c * 128:(c + 1) * 128, :])
    if "d_dw" in dbg:
        B.rd("sync", "G0")
        for c in range(KC):
            dma("sync", "sq", dbg["d_dw"][c * 128:(c + 1) * 128, :],
                g_dn[c * 128:(c + 1) * 128, :])

    gTdr = gT_dn.rearrange("(kc s) d -> s kc d", s=128)
    for half in range(2):
        B.war("sync", *(ABTOK + XYTOK + APTOK +
                        ("asc0", "asc1", "asc2")
                        + tuple(f"tbB{m}{p}" for m in range(3) for p in range(2))
                        + tuple(f"tbC{m}{p}" for m in range(3) for p in range(2))))
        B._wait("sync", "sIN0", sems["sIN0"].cnt)
        B._wait("sync", "sIN1", sems["sIN1"].cnt)
        for grp in range(8):
            src = inter_dr[grp * 1024:(grp + 1) * 1024,
                           half * 1024:(half + 1) * 1024]
            dma("sync", "sFBI", ABi[:, grp * 8:(grp + 1) * 8, :],
                src.rearrange("(ic s) t -> s ic t", s=128), w=(f"abg{grp}",))
        for do in range(KC):
            dp = do % 2
            bk = (pW[0], pW[1]) if dp == 0 else (pW[2], pW[3])
            bt = ("pW0", "pW1") if dp == 0 else ("pW2", "pW3")
            dsl = ARENA[:, 0:8192] if dp == 0 else None
            dsl = ARENA[:, 0:8192]
            dslv = dsl.rearrange("p (kc d) -> p kc d", kc=64)
            B.war("sync", *ARTOK)
            B.rd("sync", "gT0")
            dma("sync", "sFBD", dslv, gTdr[:, :, do * 128:(do + 1) * 128], w=ARTOK)
            B.rd("tensor", *ARTOK)
            B.rd("tensor", *ABTOK)
            B.war("tensor", *bt)
            for kc in range(64):
                st0 = (kc == 0); sp0 = (kc == 63)
                for n in range(2):
                    fn = (lambda e, p=bk[n], l=dslv[:, kc, :],
                          i=ABi[:, kc, n * 512:(n + 1) * 512], s0=st0, s1=sp0:
                          e.matmul(p[:], l, i, start=s0, stop=s1))
                    if sp0:
                        B.op("tensor", fn, "st", w=(bt[n],),
                             r=ARTOK + ABTOK)
                    else:
                        B.plain("tensor", lambda e, f=fn: f(e))
            xfull = XFT[:, 0:1024]
            B.war("gpsimd", "XF0")
            dma("gpsimd", "sFX", xfull,
                xmid_dr[do * 128:(do + 1) * 128, half * 1024:(half + 1) * 1024],
                w=("XF0",))
            for n in range(2):
                B.rd("vector", bt[n], "XF0")
                B.op("vector", lambda e, p=bk[n], xs=xfull[:, n * 512:(n + 1) * 512]:
                     e.tensor_add(xs, p[:], xs),
                     "sv", w=("XF0",), r=(bt[n],))
            B.rd("gpsimd", "XF0")
            dma("gpsimd", "sOT", oT[do * 128:(do + 1) * 128,
                                    half * 1024:(half + 1) * 1024],
                xfull, r=("XF0",), w=("oT",))

    assert _pump["next"] == NSUP, _pump["next"]
    for snm in ("sq", "sqG", "sqG1", "sIN0", "sIN1", "sBD", "sSG", "sOT",
                "sDBG"):
        if sems[snm].cnt:
            B.streams["sync"].append(
                lambda e, h=sems[snm].h, c=sems[snm].cnt: e.wait_ge(h, c))

    with nc.Block() as block:
        @block.sync
        def _(eng):
            for fn in B.streams["sync"]:
                fn(eng)
        @block.gpsimd
        def _(eng):
            for fn in B.streams["gpsimd"]:
                fn(eng)
        @block.vector
        def _(eng):
            for fn in B.streams["vector"]:
                fn(eng)
        @block.scalar
        def _(eng):
            for fn in B.streams["scalar"]:
                fn(eng)
        @block.tensor
        def _(eng):
            for fn in B.streams["tensor"]:
                fn(eng)
    for c in reversed(ctxs):
        c.__exit__(None, None, None)
    _NC_CACHE[key] = nc
    return nc

def kernel(L0_attn_x, L0_attn_Wq, L0_attn_Wv, L0_attn_Wg, L0_attn_Wo,
           L0_attn_beta, L0_attn_ln_w, L0_attn_ln_b,
           L0_ffn_gate_cp, L0_ffn_up_cp, L0_ffn_down_cp, _debug=()):
    tabs = _host_tables()
    x = np.ascontiguousarray(np.asarray(L0_attn_x, dtype=np.float32))
    assert x.shape == (4, 4096, 2048)
    t5g = _t5_planes(L0_ffn_gate_cp)
    t5u = _t5_planes(L0_ffn_up_cp)
    t5d = _t5_planes(L0_ffn_down_cp)
    lnw = np.ascontiguousarray(np.asarray(L0_attn_ln_w, np.float32).reshape(KC, 128).T)
    lnb = np.ascontiguousarray(np.asarray(L0_attn_ln_b, np.float32).reshape(KC, 128).T)
    shared = dict(
        wq=np.ascontiguousarray(np.asarray(L0_attn_Wq, np.float32)),
        wv=np.ascontiguousarray(np.asarray(L0_attn_Wv, np.float32)),
        wg=np.ascontiguousarray(np.asarray(L0_attn_Wg, np.float32)),
        wo=np.ascontiguousarray(np.asarray(L0_attn_Wo, np.float32)),
        beta=np.ascontiguousarray(np.asarray(L0_attn_beta, np.float32).reshape(1, H)),
        lnw=lnw, lnb=lnb, jrev=tabs["jrev"], eye=tabs["eye"],
        ones128=tabs["ones128"], onescol=tabs["onescol"],
        f1t=tabs["f1"], f2t=tabs["f2"], f3t=tabs["f3"], mskt=tabs["msk"],
        mskct=tabs["mskc"],
        t5g=t5g, t5u=t5u, t5d=t5d,
    )
    zeros_pref = np.zeros((T, D), np.float32)
    in_maps = []
    for core in range(8):
        b = core // 2; hf = core % 2
        m = dict(shared)
        m["xT"] = np.ascontiguousarray(x[b, hf * T:(hf + 1) * T, :].T)
        m["xp"] = zeros_pref if hf == 0 else np.ascontiguousarray(x[b, 0:T, :])
        in_maps.append(m)
    nc = build_nc(debug=_debug)
    res = run_bass_kernel_spmd(nc, in_maps, core_ids=list(range(8)))
    out = np.empty((4, 4096, 2048), np.float32)
    for core in range(8):
        b = core // 2; hf = core % 2
        out[b, hf * T:(hf + 1) * T, :] = res.results[core]["oT"].T
    if _debug:
        kernel._dbg = res.results
    return out


# version 14
# speedup vs baseline: 1.1756x; 1.1671x over previous
import numpy as np
import ml_dtypes
import concourse.bass as bass
import concourse.mybir as mybir
from concourse.bass_utils import run_bass_kernel_spmd

F32 = mybir.dt.float32
BF16 = mybir.dt.float16
F16 = mybir.dt.float16
U8 = mybir.dt.uint8
AF = mybir.ActivationFunctionType
ALU = mybir.AluOpType
AXX = mybir.AxisListType.X

D = 2048; T = 2048; H = 16; II = 8192
KC = D // 128
NS = II * D
NB = NS // 128
NCTRL = 130156
NT = NB // 128
NB2 = NS // 256
NT2 = NB2 // 128
NSUP = NT // 8
LN_EPS = 1e-5

_HOST = {}

def _host_tables():
    if _HOST:
        return _HOST
    t = np.linspace(0.0, NCTRL - 1.0, NS, dtype=np.float64)
    ii = np.clip(np.floor(t).astype(np.int64), 0, NCTRL - 2)
    f = (t - ii).astype(np.float32)
    f2 = f * f
    f3 = f2 * f
    iblk = ii.reshape(NS // 256, 256)
    I0 = iblk[:, 0].copy()
    assert int((iblk - I0[:, None]).max()) <= 2
    mskB = (iblk == I0[:, None] + 1).astype(np.uint8).reshape(-1)
    mskC = (iblk >= I0[:, None] + 2).astype(np.uint8).reshape(-1)
    tap_idx = np.clip(I0[:, None] + np.arange(-1, 6)[None, :], 0, NCTRL - 1)
    lam64 = (NCTRL - 1.0) / (NS - 1.0)
    i0v = np.arange(NS // 256, dtype=np.float64) * 256.0
    phi = (i0v * lam64 - I0).astype(np.float32)
    j64 = np.arange(256, dtype=np.float64) * lam64
    jrt = np.zeros((12, 768), np.float64)
    for part in range(3):
        for q in range(4):
            jrt[part * 4 + q, part * 256:(part + 1) * 256] = (j64 - part) ** q
    p = np.arange(128)[:, None]; c = np.arange(KC)[None, :]
    _HOST.update(
        f1=f, f2=f2, f3=f3, msk=mskB, mskc=mskC, tap_idx=tap_idx,
        phi=np.ascontiguousarray(phi.reshape(NT2, 128).T),
        jrt=jrt.astype(np.float32),
        jrev=np.ascontiguousarray(((T - 1) - (128 * c + p)).astype(np.float32)),
        eye=np.eye(128, dtype=np.float16),
        ones128=np.ones((128, 128), dtype=np.float16),
        onescol=np.ones((1, 128), dtype=np.float32),
    )
    return _HOST

def _t5_planes(cp):
    tabs = _host_tables()
    t5 = np.asarray(cp, dtype=np.float32)[tabs["tap_idx"]]
    return np.ascontiguousarray(t5.reshape(NT2, 128, 7).transpose(2, 1, 0))

class _Sem:
    __slots__ = ("h", "name", "cnt")
    def __init__(self, h, name):
        self.h = h; self.name = name; self.cnt = 0

class _B:
    ENGS = ("sync", "gpsimd", "vector", "scalar", "tensor")
    def __init__(self, sems, dmasems):
        self.streams = {e: [] for e in self.ENGS}
        self.sems = sems
        self.dmasems = dmasems
        self.waited = {}
        self.lastw = {}
        self.readers = {}
    def _target(self, semname, cnt):
        if semname in self.dmasems:
            return self.sems[semname].cnt
        return cnt
    def _wait(self, eng, semname, cnt):
        if cnt <= 0:
            return
        key = (eng, semname)
        if self.waited.get(key, 0) >= cnt:
            return
        self.waited[key] = cnt
        h = self.sems[semname].h
        self.streams[eng].append(lambda e, h=h, c=cnt: e.wait_ge(h, c))
    def rd(self, eng, *bufs):
        for b in bufs:
            w = self.lastw.get(b)
            if w:
                self._wait(eng, w[0], self._target(w[0], w[1]))
    def war(self, eng, *bufs):
        for b in bufs:
            w = self.lastw.get(b)
            if w:
                self._wait(eng, w[0], self._target(w[0], w[1]))
            for sn, c in self.readers.get(b, {}).items():
                self._wait(eng, sn, self._target(sn, c))
    def op(self, eng, fn, semname, inc=1, w=(), r=()):
        self.rd(eng, *r)
        self.war(eng, *w)
        s = self.sems[semname]
        s.cnt += inc
        cnt = s.cnt
        h = s.h
        self.streams[eng].append(lambda e, fn=fn, h=h, i=inc: fn(e).then_inc(h, i))
        for b in w:
            self.lastw[b] = (semname, cnt)
            self.readers[b] = {}
        for b in r:
            self.readers.setdefault(b, {})[semname] = cnt
        return cnt
    def plain(self, eng, fn):
        self.streams[eng].append(fn)

_NC_CACHE = {}

def build_nc(debug=()):
    key = ("v1", tuple(sorted(debug)))
    if key in _NC_CACHE:
        return _NC_CACHE[key]
    nc = bass.Bass("TRN2", target_bir_lowering=False)

    def inp(name, shape, dt=F32):
        return nc.dram_tensor(name, list(shape), dt, kind="ExternalInput").ap()

    xT = inp("xT", (D, T))
    xp = inp("xp", (T, D))
    wr_ = [inp("wq", (D, D)), inp("wv", (D, D)), inp("wg", (D, D)), inp("wo", (D, D))]
    beta = inp("beta", (1, H))
    lnww = inp("lnw", (128, KC)); lnbb = inp("lnb", (128, KC))
    jrev = inp("jrev", (128, KC))
    eye = inp("eye", (128, 128), BF16)
    ones128 = inp("ones128", (128, 128), BF16)
    onescol = inp("onescol", (1, 128))
    f1t = inp("f1t", (NS,)); f2t = inp("f2t", (NS,)); f3t = inp("f3t", (NS,))
    mskt = inp("mskt", (NS,), U8)
    mskct = inp("mskct", (NS,), U8)
    phit = inp("phit", (128, NT2))
    jrtt = inp("jrt", (12, 768))
    t5in = [inp("t5g", (7, 128, NT2)), inp("t5u", (7, 128, NT2)),
            inp("t5d", (7, 128, NT2))]

    oT = nc.dram_tensor("oT", [D, T], F32, kind="ExternalOutput").ap()
    dbg = {}
    _dbgdt = {"d_x1": BF16, "d_y": BF16, "d_xmid": F32, "d_x2": BF16,
              "d_g": BF16, "d_state": BF16, "d_carry": F32, "d_gw": BF16,
              "d_dw": BF16, "d_inter": BF16}
    for nm, shape in (("d_x1", (D, T)), ("d_y", (D, T)), ("d_xmid", (D, T)),
                      ("d_x2", (D, T)), ("d_g", (D, T)), ("d_state", (D, T)),
                      ("d_carry", (128, H)), ("d_gw", (II, D)), ("d_dw", (D, II)),
                      ("d_inter", (II, T))):
        if nm in debug:
            dbg[nm] = nc.dram_tensor(nm, list(shape), _dbgdt[nm],
                                     kind="ExternalOutput").ap()

    def dram(name, shape, dt):
        return nc.dram_tensor(name, list(shape), dt, kind="Internal").ap()

    w_b = [dram("wq_b", (D, D), BF16), dram("wv_b", (D, D), BF16),
           dram("wg_b", (D, D), BF16), dram("wo_b", (D, D), BF16)]
    w_bT = [dram("wq_bT", (D, D), BF16), dram("wv_bT", (D, D), BF16),
            dram("wg_bT", (D, D), BF16), dram("wo_bT", (D, D), BF16)]
    g_dn = dram("g_dn", (D, II), BF16)
    g_gate = dram("g_gate", (II, D), BF16)
    g_up = dram("g_up", (II, D), BF16)
    gT_dn = dram("g_dnT", (II, D), BF16)
    gT_gate = dram("g_gateT", (D, II), BF16)
    gT_up = dram("g_upT", (D, II), BF16)
    y_dr = dram("y_dr", (D, T), BF16)
    gg_dr = dram("gg_dr", (D, T), BF16)
    xmid_dr = dram("xmid_dr", (D, T), F32)
    xmidb_dr = dram("xmidb_dr", (D, T), BF16)
    inter_dr = dram("inter_dr", (II, T), BF16)

    ctxs = []
    def sbuf(name, shape, dt):
        t = nc.sbuf_tensor(name, list(shape), dt); ctxs.append(t); return t.__enter__()
    def psum(name, shape, dt):
        t = nc.psum_tensor(name, list(shape), dt); ctxs.append(t); return t.__enter__()

    AB = sbuf("AB", [128, 65536], BF16)
    APLr = AB[:, 0:18432].rearrange("p (c q) -> p c q", q=36)
    XY = AB[:, 24576:57344]
    def xyc(c):
        return XY[:, c * 2048:(c + 1) * 2048]
    ABi = AB.rearrange("p (ic t) -> p ic t", t=1024)
    XYTOK = tuple(f"xy{c}" for c in range(KC))
    APTOK = tuple(f"apl{q}" for q in range(36))
    ABTOK = tuple(f"abg{g}" for g in range(8))

    ARENA = sbuf("ARENA", [128, 8192], BF16)
    ARTOK = ("AR0", "AR1", "AR2", "AR3")
    MRB = sbuf("MRB", [128, 4096], F16)
    FT = sbuf("FT", [128, 2560], F16)
    MKT = sbuf("MKT", [128, 2048], U8)
    GST = sbuf("GST", [128, 4096], BF16)
    WRK = sbuf("WRK", [128, 1536], F32)
    SMA = sbuf("SMA", [128, 384], F32)
    SMB = sbuf("SMB", [128, 1024], BF16)
    EVC = sbuf("EVC", [128, 6144], BF16)
    XFT = sbuf("XFT", [128, 2048], F32)
    T5Q = XFT[:, 0:1280]
    BT = sbuf("BT", [128, 512], BF16)
    BNC = sbuf("BNC", [128, 2048], BF16)

    pW = [psum(f"pW{i}", [128, 512], F32) for i in range(4)]
    pacc = [psum(f"pA{i}", [128, 512], F32) for i in range(2)]
    pQc = psum("pQc", [128, 512], F32)
    pXs = psum("pXs", [128, 512], F16)

    DMASEMS = ("sq", "sqG", "sqG1", "sIN0", "sIN1", "sWS0", "sWS1", "sWB2", "sFX", "sOT", "sBD", "sC", "sWB0", "sWB1", "sWB2c", "sWB3", "sT5", "sF0",
               "sF1", "sMK", "sLS0", "sLS1", "sXT", "sWSL", "sYG", "sFA0",
               "sFA1", "sFBI", "sFBD", "sSG", "sDBG")
    sems = {}
    for nm in DMASEMS + ("sv", "ss", "st", "sp"):
        t = nc.semaphore(nm); ctxs.append(t)
        sems[nm] = _Sem(t.__enter__(), nm)
    B = _B(sems, set(DMASEMS))

    def dma(q, sem, out, in_, w=(), r=()):
        return B.op(q, lambda e, o=out, i=in_: e.dma_start(out=o, in_=i), sem, 16, w=w, r=r)
    def dmaT(q, sem, out, in_, w=(), r=()):
        return B.op(q, lambda e, o=out, i=in_: e.dma_start_transpose(o, i), sem, 16, w=w, r=r)

    eye_s = SMB[:, 0:128]; ones_s = SMB[:, 128:256]
    st_s = SMB[:, 256:512]
    wpb_s = [SMB[:, 512 + 16 * i:512 + 16 * (i + 1)] for i in range(2)]
    mub_s = [SMB[:, 544 + 2 + i:544 + 3 + i] for i in range(2)]
    tB2 = None
    onescol_s = SMA[0:1, 0:128]
    beta_s = SMA[0:1, 128:144]
    jrev_s = SMA[:, 160:176]
    lnw_s = SMA[:, 176:192]; lnb_s = SMA[:, 192:208]
    lam_s = SMA[:, 208:224]; lnl_s = SMA[:, 224:240]
    pmu_s = SMA[:, 240:256]; pr_s = SMA[:, 256:272]
    carry_s = SMA[:, 288:304]
    wp_s = SMA[:, 304:320]
    mu1 = MRB[:, 0:2048]; r1 = MRB[:, 2048:4096]
    eps_c = SMA[:, 320:321]; mhalf_c = SMA[:, 321:322]
    tmp0 = WRK[:, 0:512]; tmp1 = WRK[:, 512:1024]
    PHI = WRK[:, 1024:1536]
    LB = [FT[0:36, 0:128], FT[0:36, 128:256]]
    JR36 = [FT[0:36, 256 + 768 * k:256 + 768 * (k + 1)] for k in range(3)]
    qh = EVC[:, 0:2048]; vh = EVC[:, 2048:4096]; syh = EVC[:, 4096:6144]
    spbv = EVC[0:16, 2048:4096]
    wsl_ = [ARENA[:, 0:2048], ARENA[:, 2048:4096]]
    EQ = ("Eq0", "Eq1"); EV = ("Ev0", "Ev1"); ES = ("Es0", "Es1")

    B.op("vector", lambda e: e.memset(eps_c, LN_EPS), "sv", w=("epsc",))
    B.op("vector", lambda e: e.memset(mhalf_c, -0.5), "sv", w=("epsc",))
    dma("sync", "sC", eye_s, eye, w=("eye",))
    dma("sync", "sC", ones_s, ones128, w=("ones",))
    dma("sync", "sC", onescol_s, onescol, w=("onescol",))
    dma("sync", "sC", beta_s, beta, w=("beta",))
    dma("sync", "sC", jrev_s, jrev, w=("jrev",))
    dma("sync", "sC", lnw_s, lnww, w=("lnw",))
    dma("sync", "sC", lnb_s, lnbb, w=("lnb",))

    B.rd("tensor", "onescol", "beta")
    B.op("tensor", lambda e: e.matmul(pacc[0][:, 0:H], onescol_s, beta_s,
                                      start=True, stop=True), "st", w=("pacc0",))
    B.rd("scalar", "pacc0")
    B.op("scalar", lambda e: e.activation(lam_s, pacc[0][:, 0:H], AF.Sigmoid),
         "ss", w=("lam",), r=("pacc0",))
    B.rd("vector", "lam")
    B.op("vector", lambda e: e.tensor_single_scalar(lnl_s, lam_s, 1e-38, ALU.max),
         "sv", w=("lnl",))
    B.op("scalar", lambda e: e.activation(lnl_s, lnl_s, AF.Ln), "ss",
         w=("lnl",), r=("lnl",))

    ACOEF = [[(1, 1.0)],
             [(0, -0.5), (2, 0.5)],
             [(0, 1.0), (1, -2.5), (2, 2.0), (3, -0.5)],
             [(0, -0.5), (1, 1.5), (2, -1.5), (3, 0.5)]]
    NQ = NT2 // 2
    for mi in range(3):
        for qb in range(2):
            c0 = qb * NQ
            t5v = [T5Q[:, j * NQ:(j + 1) * NQ] for j in range(5)]
            t5w = [XFT[:, 1280 + jj * NQ:1280 + (jj + 1) * NQ] for jj in range(2)]
            for j in range(7):
                B.war("sync", "XF0", "XF1")
                dst = t5v[j] if j < 5 else t5w[j - 5]
                dma("sync", "sT5", dst, t5in[mi][j, :, c0:c0 + NQ],
                    w=("XF0", "XF1") if j == 6 else ())
            t5all = t5v + t5w
            for part in range(3):
                for q in range(4):
                    terms = [(tap + part, co) for tap, co in ACOEF[q]]
                    dq = mi * 12 + part * 4 + q
                    dst = APLr[:, c0:c0 + NQ, dq]
                    for j, (tap, co) in enumerate(terms):
                        last = (j == len(terms) - 1)
                        o = dst if last else (tmp0[:, 0:NQ] if j % 2 == 0 else tmp1[:, 0:NQ])
                        wtok = (f"apl{dq}",) if last else ("t0" if j % 2 == 0 else "t1",)
                        B.war("vector", *wtok)
                        if j == 0:
                            B.op("vector", lambda e, o=o, i=t5all[tap], co=co:
                                 e.tensor_single_scalar(o, i, co, ALU.mult),
                                 "sv", w=wtok, r=("XF0",))
                        else:
                            prev = tmp0[:, 0:NQ] if (j - 1) % 2 == 0 else tmp1[:, 0:NQ]
                            ptok = "t0" if (j - 1) % 2 == 0 else "t1"
                            B.rd("vector", ptok)
                            B.op("vector", lambda e, o=o, i=t5all[tap], co=co, pv=prev:
                                 e.scalar_tensor_tensor(out=o, in0=i, scalar=co, in1=pv,
                                                        op0=ALU.mult, op1=ALU.add),
                                 "sv", w=wtok, r=("XF0", ptok))

    dma("sync", "sC", PHI, phit, w=("phi",))
    B.op("vector", lambda e: e.memset(FT[0:36, 256:2560], 0.0), "sv",
         w=("JR",))
    B.rd("gpsimd", "JR")
    for mi in range(3):
        dma("gpsimd", "sF0", JR36[mi][12 * mi:12 * mi + 12, :], jrtt,
            w=("JR",), r=("JR",))
    for mi in range(3):
        for part in range(3):
            dq = mi * 12 + part * 4
            pl = [APLr[:, :, dq + q] for q in range(4)]
            ptk = [f"apl{dq + q}" for q in range(4)]
            B.rd("vector", "phi", *ptk)
            B.war("vector", "t1")
            B.op("vector", lambda e, a3=pl[3]: e.tensor_mul(tmp1, a3, PHI),
                 "sv", w=("t1",), r=("phi", ptk[3]))
            B.war("vector", "t0")
            B.op("vector", lambda e, a2=pl[2]: e.tensor_add(tmp0, tmp1, a2),
                 "sv", w=("t0",), r=("t1", ptk[2]))
            B.op("vector", lambda e: e.tensor_mul(tmp0, tmp0, PHI),
                 "sv", w=("t0",), r=("t0", "phi"))
            B.op("vector", lambda e, a1=pl[1]: e.tensor_add(tmp0, tmp0, a1),
                 "sv", w=("t0",), r=("t0", ptk[1]))
            B.op("vector", lambda e: e.tensor_mul(tmp0, tmp0, PHI),
                 "sv", w=("t0",), r=("t0", "phi"))
            B.op("vector", lambda e, a0=pl[0]: e.tensor_add(a0, tmp0, a0),
                 "sv", w=(ptk[0],), r=("t0", ptk[0]))
            B.op("vector", lambda e, a2=pl[2]:
                 e.scalar_tensor_tensor(out=tmp0, in0=tmp1, scalar=1.5, in1=a2,
                                        op0=ALU.mult, op1=ALU.add),
                 "sv", w=("t0",), r=("t1", ptk[2]))
            B.op("vector", lambda e: e.tensor_mul(tmp0, tmp0, PHI),
                 "sv", w=("t0",), r=("t0", "phi"))
            B.op("vector", lambda e, a1=pl[1]:
                 e.scalar_tensor_tensor(out=a1, in0=tmp0, scalar=2.0, in1=a1,
                                        op0=ALU.mult, op1=ALU.add),
                 "sv", w=(ptk[1],), r=("t0", ptk[1]))
            B.op("vector", lambda e, a2=pl[2]:
                 e.scalar_tensor_tensor(out=a2, in0=tmp1, scalar=3.0, in1=a2,
                                        op0=ALU.mult, op1=ALU.add),
                 "sv", w=(ptk[2],), r=("t1", ptk[2]))

    _wsem = ("sWB0", "sWB1", "sWB2c", "sWB3")
    _cast_state = {"k": 0}
    _cast_order = [(wi, c8) for wi in (1, 0, 2, 3) for c8 in range(4)]
    def cast_pump(n):
        k = _cast_state["k"]
        for j in range(k, min(k + n, 16)):
            wi, c8 = _cast_order[j]
            dma("gpsimd", _wsem[wi], w_b[wi][512 * c8:512 * (c8 + 1), :],
                wr_[wi][512 * c8:512 * (c8 + 1), :],
                w=(f"wb{wi}",) if c8 == 3 else ())
        _cast_state["k"] = min(k + n, 16)

    def build_wT(wi, c0, nc_):
        B.rd("scalar", f"wb{wi}")
        for c in range(c0, c0 + nc_):
            for hf in range(2):
                buf = BNC[:, hf * 1024:hf * 1024 + 1024]
                B.war("scalar", f"bnc{hf}")
                B.op("scalar", lambda e, o=buf,
                     i=w_b[wi][1024 * hf:1024 * (hf + 1), c * 128:(c + 1) * 128]:
                     e.dma_start_transpose(o, i), "sWB2", 16, w=(f"bnc{hf}",))
                B.rd("scalar", f"bnc{hf}")
                B.op("scalar", lambda e,
                     o=w_bT[wi][c * 128:(c + 1) * 128, 1024 * hf:1024 * (hf + 1)],
                     i=buf:
                     e.dma_start(out=o, in_=i), "sWB2", 16,
                     r=(f"bnc{hf}",), w=(f"wT{wi}",))

    def flat4(t):
        return t.rearrange("(s tj p j) -> s p tj j", tj=4, p=128, j=256)
    mkf = flat4(mskt)
    mkcf = flat4(mskct)
    gmats = (g_dn, g_gate, g_up)
    gmi = (2, 0, 1)
    gfl = [flat4(g.rearrange("a b -> (a b)")) for g in gmats]
    gw_cnt = [dict() for _ in range(3)]
    _blu = {"u": 0}

    def blend_supertile(s):
        bi = s % 2
        mkb = MKT[:, 0:1024]
        mkc = MKT[:, 1024:2048]
        fv = lambda b, tj: b.rearrange("p (tj j) -> p tj j", j=256)[:, tj, :]
        B.war("gpsimd", "mk")
        dma("gpsimd", "sMK", mkb.rearrange("p (tj j) -> p tj j", j=256),
            mkf[s], w=("mk",))
        dma("gpsimd", "sMK", mkc.rearrange("p (tj j) -> p tj j", j=256),
            mkcf[s], w=("mk",))
        for tj in range(4):
            c0 = 4 * s + tj
            tpar = tj % 2
            lb = LB[tpar]
            btok = f"lb{tpar}"
            B.rd("tensor", "eye", *APTOK)
            B.war("tensor", "pXs")
            B.op("tensor", lambda e, i=AB[:, c0 * 36:c0 * 36 + 36]:
                 e.transpose(pXs[0:36, 0:128], i, eye_s),
                 "st", w=("pXs",), r=APTOK + ("eye",))
            B.rd("scalar", "pXs")
            B.war("scalar", btok)
            B.op("scalar", lambda e, o=lb:
                 e.copy(out=o, in_=pXs[0:36, 0:128]),
                 "ss", w=(btok,), r=("pXs",))
            mkj = fv(mkb, tj); mkjc = fv(mkc, tj)
            for m in range(3):
                mi = gmi[m]
                u = _blu["u"]; _blu["u"] += 1
                par = u % 2
                pj = pacc[par]; pjt = f"pacc{par}"
                slot = (bi if m == 0 else (2 + bi if m == 1 else 1 - bi))
                gs = GST[:, slot * 1024 + tj * 256:
                         slot * 1024 + tj * 256 + 256]
                gtok = f"gs{slot}_{tj}"
                B.rd("tensor", btok, "JR")
                B.war("tensor", pjt)
                B.op("tensor", lambda e, p=pj, l=lb,
                     jr=JR36[mi][:, 0:512]:
                     e.matmul(p[:, 0:512], l, jr, start=True, stop=True),
                     "st", w=(pjt,), r=(btok, "JR"))
                B.war("tensor", "pQc")
                B.op("tensor", lambda e, l=lb, jr=JR36[mi][:, 512:768]:
                     e.matmul(pQc[:, 0:256], l, jr, start=True, stop=True),
                     "st", w=("pQc",), r=(btok, "JR"))
                B.rd("vector", pjt, "mk")
                B.war("vector", pjt)
                B.op("vector", lambda e, p=pj, mk=mkj:
                     e.copy_predicated(out=p[:, 0:256], mask=mk,
                                       data=p[:, 256:512]),
                     "sv", w=(pjt,), r=(pjt, "mk"))
                B.rd("vector", "pQc")
                B.op("vector", lambda e, p=pj, mk=mkjc:
                     e.copy_predicated(out=p[:, 0:256], mask=mk,
                                       data=pQc[:, 0:256]),
                     "sv", w=(pjt,), r=("pQc", "mk", pjt))
                B.rd("scalar", pjt)
                B.op("scalar", lambda e, o=gs, p=pj:
                     e.copy(out=o, in_=p[:, 0:256]),
                     "ss", w=(gtok,), r=(pjt,))
        for m in range(3):
            slot = (bi if m == 0 else (2 + bi if m == 1 else 1 - bi))
            gtoks = tuple(f"gs{slot}_{tj}" for tj in range(4))
            B.rd("sync", *gtoks)
            gs = GST[:, slot * 1024:slot * 1024 + 1024]
            c = dma("sync", "sqG" if s % 2 == 0 else "sqG1", gfl[m][s],
                    gs.rearrange("p (tj j) -> p tj j", j=256),
                    r=gtoks, w=(f"G{m}",) + gtoks)
            gw_cnt[m][s] = c

    def bounce_gu_kilo(k):
        for m, gsrc, gdst in ((1, g_gate, gT_gate), (2, g_up, gT_up)):
            assert (16 * k + 15) in gw_cnt[m]
            B._wait("sync", "sqG", sems["sqG"].cnt)
            B._wait("sync", "sqG1", sems["sqG1"].cnt)
            for p0 in range(16):
                bi = p0 % 2
                buf = BNC[:, bi * 1024:bi * 1024 + 1024]
                B.war("sync", f"bnc{bi}")
                dmaT("sync", "sqG", buf,
                     gsrc[1024 * k:1024 * (k + 1), 128 * p0:128 * (p0 + 1)],
                     w=(f"bnc{bi}",))
                B.rd("sync", f"bnc{bi}")
                dma("sync", "sqG",
                    gdst[128 * p0:128 * (p0 + 1), 1024 * k:1024 * (k + 1)],
                    buf, r=(f"bnc{bi}",), w=(f"gT{m}k{k}",))

    def bounce_dn_pairs(p0, n):
        if p0 == 0:
            assert (NSUP - 1) in gw_cnt[0]
            B._wait("scalar", "sqG", sems["sqG"].cnt)
            B._wait("scalar", "sqG1", sems["sqG1"].cnt)
        for p in range(p0, p0 + n):
            ic = p // 2; hf = p % 2
            buf = BNC[:, hf * 1024:hf * 1024 + 1024]
            B.war("scalar", f"bnc{hf}")
            B.op("scalar", lambda e, o=buf, i=g_dn[1024 * hf:1024 * (hf + 1),
                                                   128 * ic:128 * (ic + 1)]:
                 e.dma_start_transpose(o, i), "sBD", 16, w=(f"bnc{hf}",))
            B.rd("scalar", f"bnc{hf}")
            B.op("scalar", lambda e, o=gT_dn[128 * ic:128 * (ic + 1),
                                             1024 * hf:1024 * (hf + 1)], i=buf:
                 e.dma_start(out=o, in_=i), "sBD", 16, r=(f"bnc{hf}",), w=("gT0",))

    _pump = {"next": 0}
    def pump(n):
        s0 = _pump["next"]
        for s in range(s0, min(s0 + n, NSUP)):
            blend_supertile(s)
            if s % 16 == 15:
                bounce_gu_kilo(s // 16)
        _pump["next"] = min(s0 + n, NSUP)

    def ln_stats(src_dr, tag, bf16src):
        for half in range(2):
            for c in range(KC):
                sl = c % 2
                src = src_dr[c * 128:(c + 1) * 128, half * 1024:(half + 1) * 1024]
                xb = EVC[:, sl * 1024:sl * 1024 + 1024]
                if bf16src:
                    B.war("sync", EQ[sl])
                    dma("sync", "sLS0" if sl == 0 else "sLS1", xb, src, w=(EQ[sl],))
                else:
                    xf = XFT[:, sl * 1024:sl * 1024 + 1024]
                    B.war("sync", f"XF{sl}")
                    dma("sync", "sLS0" if sl == 0 else "sLS1", xf, src, w=(f"XF{sl}",))
                    B.rd("gpsimd", f"XF{sl}")
                    B.war("gpsimd", EQ[sl])
                    B.op("gpsimd", lambda e, o=xb, i=xf: e.tensor_copy(out=o, in_=i),
                         "sp", w=(EQ[sl],), r=(f"XF{sl}",))
                xq = EVC[:, 2048 + sl * 1024:2048 + sl * 1024 + 1024]
                B.rd("scalar", EQ[sl])
                B.war("scalar", EV[sl])
                B.op("scalar", lambda e, o=xq, i=xb: e.activation(o, i, AF.Square),
                     "ss", w=(EV[sl],), r=(EQ[sl],))
                st0 = (c == 0); sp0 = (c == KC - 1)
                B.rd("tensor", "ones", EQ[sl], EV[sl])
                if st0:
                    B.war("tensor", "pW0", "pW1", "pW2", "pW3")
                for j in range(2):
                    fn1 = lambda e, p=pW[j], i=xb[:, j * 512:(j + 1) * 512], s0=st0, s1=sp0: \
                        e.matmul(p[:], ones_s, i, start=s0, stop=s1)
                    fn2 = lambda e, p=pW[2 + j], i=xq[:, j * 512:(j + 1) * 512], s0=st0, s1=sp0: \
                        e.matmul(p[:], ones_s, i, start=s0, stop=s1)
                    if sp0:
                        B.op("tensor", fn1, "st", w=(f"pW{j}",), r=(EQ[sl],))
                        B.op("tensor", fn2, "st", w=(f"pW{2 + j}",), r=(EV[sl],))
                    elif j == 1:
                        B.plain("tensor", lambda e, f=fn1: f(e))
                        B.op("tensor", fn2, "st", r=(EQ[sl], EV[sl]))
                    else:
                        B.plain("tensor", lambda e, f=fn1: f(e))
                        B.plain("tensor", lambda e, f=fn2: f(e))
            for j in range(2):
                tok0 = half * 1024 + j * 512
                mu_sl = mu1[:, tok0:tok0 + 512]; r_sl = r1[:, tok0:tok0 + 512]
                B.rd("vector", f"pW{j}", f"pW{2 + j}")
                B.war("vector", "t0", "t1", "mu_c")
                B.op("vector", lambda e, p=pW[j]:
                     e.tensor_single_scalar(tmp0, p[:], 1.0 / D, ALU.mult),
                     "sv", w=("t0",), r=(f"pW{j}",))
                B.op("vector", lambda e, m=mu_sl: e.tensor_copy(out=m, in_=tmp0),
                     "sv", w=("mu_c",), r=("t0",))
                B.op("vector", lambda e: e.tensor_mul(tmp1, tmp0, tmp0),
                     "sv", w=("t1",), r=("t0",))
                B.op("vector", lambda e, p=pW[2 + j]:
                     e.scalar_tensor_tensor(out=tmp0, in0=p[:], scalar=1.0 / D,
                                            in1=tmp1, op0=ALU.mult, op1=ALU.subtract),
                     "sv", w=("t0",), r=(f"pW{2 + j}", "t1"))
                B.rd("scalar", "t0", "epsc")
                B.war("scalar", "r_c")
                B.op("scalar", lambda e: e.activation(tmp0, tmp0, AF.Ln, bias=eps_c),
                     "ss", w=("t0",), r=("t0", "epsc"))
                B.op("scalar", lambda e, r=r_sl: e.activation(r, tmp0, AF.Exp,
                                                              scale=mhalf_c),
                     "ss", w=("r_c",), r=("t0",))

    for k in range(8):
        pump(1)
        cast_pump(2)
    for wi in (1, 0):
        for c0 in (0, 8):
            pump(1)
            build_wT(wi, c0, 8)
    ln_stats(xT, "l1", False)
    for wi in (2, 3):
        for c0 in (0, 8):
            pump(1)
            build_wT(wi, c0, 8)

    for c in range(KC):
        sl = c % 2
        xfull = XFT[:, 0:2048]
        B.war("sync", "XF0", "XF1")
        dma("sync", "sXT", xfull, xp[c * 128:(c + 1) * 128, :], w=("XF0", "XF1"))
        xb = EVC[:, 0:2048]
        B.rd("gpsimd", "XF0")
        B.war("gpsimd", *EQ)
        B.op("gpsimd", lambda e, o=xb, i=xfull: e.tensor_copy(out=o, in_=i),
             "sp", w=EQ, r=("XF0", "XF1"))
        xq = EVC[:, 2048:4096]
        B.rd("scalar", "XF0")
        B.war("scalar", *EV)
        B.op("scalar", lambda e, o=xq, i=xfull: e.activation(o, i, AF.Square),
             "ss", w=EV, r=("XF0", "XF1"))
        B.rd("vector", "XF0", *EV)
        B.war("vector", "pst0", "pst1", "t0")
        B.op("vector", lambda e, o=pmu_s[:, c:c + 1], i=xfull:
             e.reduce_sum(o, i, axis=AXX), "sv", w=("pst0",), r=("XF0", "XF1"))
        B.op("vector", lambda e, o=pr_s[:, c:c + 1], i=xq:
             e.reduce_sum(o, i, axis=AXX), "sv", w=("pst1",), r=EV)
        B.op("vector", lambda e, o=pmu_s[:, c:c + 1]:
             e.tensor_single_scalar(o, o, 1.0 / D, ALU.mult), "sv", w=("pst0",),
             r=("pst0",))
        B.op("vector", lambda e, o=tmp0[:, 0:1], m=pmu_s[:, c:c + 1]:
             e.tensor_mul(o, m, m), "sv", w=("t0",), r=("pst0",))
        B.op("vector", lambda e, o=pr_s[:, c:c + 1], t=tmp0[:, 0:1]:
             e.scalar_tensor_tensor(out=o, in0=o, scalar=1.0 / D, in1=t,
                                    op0=ALU.mult, op1=ALU.subtract), "sv",
             w=("pst1",), r=("t0", "pst1"))
        B.rd("scalar", "pst1", "epsc")
        B.op("scalar", lambda e, o=pr_s[:, c:c + 1]:
             e.activation(o, o, AF.Ln, bias=eps_c), "ss", w=("pst1",),
             r=("pst1", "epsc"))
        B.op("scalar", lambda e, o=pr_s[:, c:c + 1]:
             e.activation(o, o, AF.Exp, scale=mhalf_c), "ss", w=("pst1",),
             r=("pst1",))
        B.rd("scalar", "lnl", "jrev")
        B.war("scalar", "wp")
        B.op("scalar", lambda e, c_=c: e.activation(wp_s, lnl_s, AF.Exp,
                                                    scale=jrev_s[:, c_:c_ + 1]),
             "ss", w=("wp",))
        B.rd("vector", "wp", "pst1", "pst0")
        B.war("vector", f"wpb{sl}", f"mub{sl}")
        B.op("vector", lambda e, o=wpb_s[sl], c_=c:
             e.tensor_scalar(out=o, in0=wp_s, scalar1=pr_s[:, c_:c_ + 1],
                             scalar2=None, op0=ALU.mult),
             "sv", w=(f"wpb{sl}",), r=("wp", "pst1"))
        B.op("vector", lambda e, o=mub_s[sl], c_=c:
             e.tensor_copy(out=o, in_=pmu_s[:, c_:c_ + 1]),
             "sv", w=(f"mub{sl}",), r=("pst0",))
        st0 = (c == 0); sp0 = (c == KC - 1)
        B.rd("tensor", f"wpb{sl}", f"mub{sl}", *EQ)
        if st0:
            B.war("tensor", "pW0", "pW1", "pW2", "pW3", "pacc0")
        for n in range(4):
            fn = lambda e, p=pW[n], l=wpb_s[sl], i=xb[:, n * 512:(n + 1) * 512], \
                s0=st0, s1=sp0: e.matmul(p[0:16, :], l, i, start=s0, stop=s1)
            if sp0:
                B.op("tensor", fn, "st", w=(f"pW{n}",), r=EQ + (f"wpb{sl}",))
            else:
                B.plain("tensor", lambda e, f=fn: f(e))
        fnm = lambda e, l=wpb_s[sl], i=mub_s[sl], s0=st0, s1=sp0: \
            e.matmul(pacc[0][0:16, 1:2], l, i, start=s0, stop=s1)
        if sp0:
            B.op("tensor", fnm, "st", w=("pacc0",), r=(f"mub{sl}", f"wpb{sl}"))
        else:
            B.op("tensor", fnm, "st", r=(EQ[0], f"wpb{sl}", f"mub{sl}"))
    B.rd("vector", "pacc0")
    B.war("vector", "t0")
    B.op("vector", lambda e: e.tensor_copy(out=tmp0[0:16, 0:1], in_=pacc[0][0:16, 1:2]),
         "sv", w=("t0",), r=("pacc0",))
    B.war("vector", *EV)
    for n in range(4):
        B.rd("vector", f"pW{n}", "t0")
        B.op("vector", lambda e, n=n, p=pW[n]:
             e.tensor_scalar(out=spbv[:, n * 512:(n + 1) * 512], in0=p[0:16, :],
                             scalar1=tmp0[0:16, 0:1], scalar2=None, op0=ALU.subtract),
             "sv", w=(EV[n // 2],), r=(f"pW{n}", "t0"))
    for c in range(KC):
        B.op("tensor", lambda e, c_=c:
             e.transpose(pXs[:, 0:16], spbv[:, c_ * 128:(c_ + 1) * 128],
                         eye_s[0:16, 0:16]),
             "st", w=("pXs",), r=EV)
        B.op("scalar", lambda e, c_=c:
             e.copy(out=st_s[:, c_ * 16:(c_ + 1) * 16], in_=pXs[:, 0:16]),
             "ss", w=("st_s",), r=("pXs",))
    pump(4)

    for c in range(KC):
        xfull = XFT[:, 0:2048]
        B.war("sync", "XF0", "XF1")
        dma("sync", "sXT", xfull, xT[c * 128:(c + 1) * 128, :], w=("XF0", "XF1"))
        x1c = xyc(c)
        B.war("vector", f"xy{c}")
        for n in range(4):
            xtok = "XF0" if n < 2 else "XF1"
            B.rd("vector", xtok, "mu_c", "r_c")
            B.war("vector", "t0")
            B.op("vector", lambda e, i=xfull[:, n * 512:(n + 1) * 512],
                 m=mu1[:, n * 512:(n + 1) * 512]:
                 e.tensor_sub(tmp0, i, m), "sv", w=("t0",), r=(xtok, "mu_c"))
            B.op("vector", lambda e, o=x1c[:, n * 512:(n + 1) * 512],
                 r_=r1[:, n * 512:(n + 1) * 512]:
                 e.tensor_mul(o, tmp0, r_), "sv", w=(f"xy{c}",), r=("t0", "r_c"))
        if c % 2 == 1:
            pump(1)
    if "d_x1" in dbg:
        for c in range(KC):
            B.rd("sync", f"xy{c}")
            dma("sync", "sq", dbg["d_x1"][c * 128:(c + 1) * 128, :], xyc(c),
                r=(f"xy{c}",))

    def w_sweep(widx, head, carry, banks, btoks):
        par = head % 2
        wsl = wsl_[par]
        wslv = wsl.rearrange("p (c a) -> p c a", c=KC)
        B.rd("sync", f"wT{widx}")
        B.war("sync", f"AR{par}")
        src_ap = w_bT[widx].rearrange("(c p) a -> p c a", p=128)[
            :, :, head * 128:(head + 1) * 128]
        dma("sync", "sWS0" if par == 0 else "sWS1", wslv, src_ap,
            w=(f"AR{par}",))
        B.rd("tensor", f"AR{par}", *XYTOK)
        B.war("tensor", *btoks)
        if carry:
            B.war("tensor", "pQc")
            B.rd("tensor", "st_s")
        for c in range(KC):
            st0 = (c == 0); sp0 = (c == KC - 1)
            for n in range(4):
                fn = (lambda e, p=banks[n], wt=wslv[:, c, :],
                      i=xyc(c)[:, n * 512:(n + 1) * 512],
                      s0=st0, s1=sp0: e.matmul(p[:], wt, i, start=s0, stop=s1))
                if sp0:
                    B.op("tensor", fn, "st", w=(btoks[n],), r=(f"AR{par}",))
                elif n == 3:
                    B.op("tensor", fn, "st", r=(f"AR{par}", f"xy{c}"))
                else:
                    B.plain("tensor", lambda e, f=fn: f(e))
            if carry:
                fnc = (lambda e, wt=wslv[:, c, :], c_=c, h_=head, s0=st0, s1=sp0:
                       e.matmul(pQc[:, 0:1], wt,
                                st_s[:, c_ * 16 + h_:c_ * 16 + h_ + 1],
                                start=s0, stop=s1))
                if sp0:
                    B.op("tensor", fnc, "st", w=("pQc",), r=("st_s", f"AR{par}"))
                else:
                    B.op("tensor", fnc, "st", r=("st_s", f"AR{par}"))

    BK_V = (pW[0], pW[1], pW[2], pW[3])
    BT_V = ("pW0", "pW1", "pW2", "pW3")
    BK_Q = BK_V
    BT_Q = BT_V
    for head in range(H):
        w_sweep(1, head, True, BK_V, BT_V)
        B.rd("scalar", "pQc")
        B.op("scalar", lambda e, h_=head:
             e.copy(out=carry_s[:, h_:h_ + 1], in_=pQc[:, 0:1]),
             "ss", w=("carry",), r=("pQc",))
        pump(3)
        for n in range(4):
            B.op("scalar", lambda e, n=n, p=BK_V[n]:
                 e.copy(out=vh[:, n * 512:(n + 1) * 512], in_=p[:]),
                 "ss", w=(EV[n // 2],), r=(BT_V[n],))
        lam_bc = lam_s[:, head:head + 1].broadcast_to([128, 2048])
        B.op("vector", lambda e, h_=head, lb=lam_bc:
             e.tensor_tensor_scan(out=syh, data0=lb, data1=vh,
                                  initial=carry_s[:, h_:h_ + 1],
                                  op0=ALU.mult, op1=ALU.add),
             "sv", w=ES, r=EV + ES + ("carry", "lam"))
        w_sweep(0, head, False, BK_Q, BT_Q)
        pump(3)
        for n in range(4):
            B.op("scalar", lambda e, n=n, p=BK_Q[n]:
                 e.copy(out=qh[:, n * 512:(n + 1) * 512], in_=p[:]),
                 "ss", w=(EQ[n // 2],), r=(BT_Q[n],))
        if "d_state" in dbg:
            B.rd("sync", *ES)
            dma("sync", "sq", dbg["d_state"][head * 128:(head + 1) * 128, :],
                syh, r=ES)
        B.op("vector", lambda e: e.tensor_mul(syh, qh, syh), "sv",
             w=ES, r=EQ + ES)
        B.rd("sync", *ES)
        dma("sync", "sq", y_dr[head * 128:(head + 1) * 128, :], syh, r=ES,
            w=("y_dr",))
    if "d_carry" in dbg:
        B.rd("sync", "carry")
        dma("sync", "sq", dbg["d_carry"], carry_s, r=("carry",))

    for head in range(H):
        bs, bt = BK_V, BT_V
        w_sweep(2, head, False, bs, bt)
        pump(1)
        for n in range(4):
            B.op("scalar", lambda e, n=n, p=bs[n]:
                 e.copy(out=qh[:, n * 512:(n + 1) * 512], in_=p[:]),
                 "ss", w=(EQ[n // 2],), r=(bt[n],))
        B.rd("sync", *EQ)
        dma("sync", "sq", gg_dr[head * 128:(head + 1) * 128, :], qh, r=EQ,
            w=("gg_dr",))
    if "d_g" in dbg:
        B.rd("sync", "gg_dr")
        for c in range(KC):
            dma("sync", "sq", dbg["d_g"][c * 128:(c + 1) * 128, :],
                gg_dr[c * 128:(c + 1) * 128, :])
    if "d_y" in dbg:
        B.rd("sync", "y_dr")
        for c in range(KC):
            dma("sync", "sq", dbg["d_y"][c * 128:(c + 1) * 128, :],
                y_dr[c * 128:(c + 1) * 128, :])

    B.rd("sync", "y_dr")
    ln_stats(y_dr, "ly", True)
    pump(4)

    for c in range(KC):
        B.war("sync", *EQ)
        B.rd("sync", "y_dr")
        dma("sync", "sYG", qh, y_dr[c * 128:(c + 1) * 128, :], w=EQ)
        B.war("sync", *EV)
        B.rd("sync", "gg_dr")
        dma("sync", "sYG", vh, gg_dr[c * 128:(c + 1) * 128, :], w=EV)
        B.rd("scalar", *EV)
        B.war("scalar", *ES)
        B.op("scalar", lambda e: e.activation(syh, vh, AF.Sigmoid), "ss",
             w=ES, r=EV)
        y2c = xyc(c)
        B.war("vector", f"xy{c}")
        for n in range(4):
            sl_ = slice(n * 512, (n + 1) * 512)
            B.rd("vector", *EQ)
            B.rd("vector", *ES)
            B.rd("vector", "mu_c", "r_c", "lnw", "lnb")
            B.war("vector", "t0")
            B.op("vector", lambda e, i=qh[:, sl_], m=mu1[:, sl_]:
                 e.tensor_sub(tmp0, i, m), "sv", w=("t0",), r=EQ + ("mu_c",))
            B.op("vector", lambda e, r_=r1[:, sl_]:
                 e.tensor_mul(tmp0, tmp0, r_), "sv", w=("t0",), r=("r_c", "t0"))
            B.op("vector", lambda e, c_=c:
                 e.tensor_scalar(out=tmp0, in0=tmp0, scalar1=lnw_s[:, c_:c_ + 1],
                                 scalar2=lnb_s[:, c_:c_ + 1], op0=ALU.mult,
                                 op1=ALU.add), "sv", w=("t0",), r=("t0", "lnw", "lnb"))
            B.op("vector", lambda e, s=syh[:, sl_]:
                 e.tensor_mul(tmp0, tmp0, s), "sv", w=("t0",), r=ES + ("t0",))
            B.op("vector", lambda e, o=y2c[:, sl_], g=vh[:, sl_]:
                 e.tensor_mul(o, tmp0, g), "sv", w=(f"xy{c}",), r=("t0",) + EV)
        pump(1)

    for head in range(H):
        bs, bt = BK_V, BT_V
        w_sweep(3, head, False, bs, bt)
        pump(1)
        xfull = XFT[:, 0:2048]
        B.war("sync", "XF0", "XF1")
        dma("sync", "sXT", xfull, xT[head * 128:(head + 1) * 128, :], w=("XF0", "XF1"))
        for n in range(4):
            xtok = "XF0" if n < 2 else "XF1"
            B.rd("vector", bt[n], xtok)
            B.op("vector", lambda e, p=bs[n], xs=xfull[:, n * 512:(n + 1) * 512]:
                 e.tensor_add(xs, p[:], xs),
                 "sv", w=("XF0", "XF1") if n == 3 else (xtok,), r=(bt[n],))
        B.rd("sync", "XF0", "XF1")
        dma("sync", "sq", xmid_dr[head * 128:(head + 1) * 128, :], xfull,
            r=("XF0", "XF1"), w=("xmid",))
        B.rd("gpsimd", "XF0", "XF1")
        dma("gpsimd", "sSG", xmidb_dr[head * 128:(head + 1) * 128, :], xfull,
            r=("XF0", "XF1"), w=("xmidb",))
        pump(1)
    if "d_xmid" in dbg:
        B.rd("sync", "xmid")
        for c in range(KC):
            dma("sync", "sq", dbg["d_xmid"][c * 128:(c + 1) * 128, :],
                xmid_dr[c * 128:(c + 1) * 128, :])

    B.rd("sync", "xmidb")
    ln_stats(xmidb_dr, "l2", True)
    pump(4)
    for c in range(KC):
        B.war("sync", *EQ)
        dma("sync", "sYG", qh, xmidb_dr[c * 128:(c + 1) * 128, :], w=EQ)
        x2c = xyc(c)
        B.war("vector", f"xy{c}")
        for n in range(4):
            sl_ = slice(n * 512, (n + 1) * 512)
            B.rd("vector", *EQ)
            B.rd("vector", "mu_c", "r_c")
            B.war("vector", "t0")
            B.op("vector", lambda e, i=qh[:, sl_], m=mu1[:, sl_]:
                 e.tensor_sub(tmp0, i, m), "sv", w=("t0",), r=EQ + ("mu_c",))
            B.op("vector", lambda e, o=x2c[:, sl_], r_=r1[:, sl_]:
                 e.tensor_mul(o, tmp0, r_), "sv", w=(f"xy{c}",), r=("t0", "r_c"))
        pump(1)
    if "d_x2" in dbg:
        for c in range(KC):
            B.rd("sync", f"xy{c}")
            dma("sync", "sq", dbg["d_x2"][c * 128:(c + 1) * 128, :], xyc(c),
                r=(f"xy{c}",))
    pump(NSUP)

    gTgr = gT_gate.rearrange("(a s) i -> s a i", s=128)
    gTur = gT_up.rearrange("(a s) i -> s a i", s=128)
    sgv_ = [EVC[:, 0:512], EVC[:, 2048:2560]]
    gcv_ = [EVC[:, 512:1024], EVC[:, 2560:3072]]
    ttv_ = [EVC[:, 1024:1536], EVC[:, 3072:3584]]
    itv_ = [EVC[:, 1536:2048], EVC[:, 3584:4096]]
    for rng in range(64):
        par = rng % 2
        gsl = ARENA[:, par * 2048:par * 2048 + 2048]
        usl = ARENA[:, 4096 + par * 2048:4096 + par * 2048 + 2048]
        gslv = gsl.rearrange("p (a i) -> p a i", a=16)
        uslv = usl.rearrange("p (a i) -> p a i", a=16)
        gt = ARTOK[par]; ut = ARTOK[2 + par]
        B.war("sync", gt)
        B.rd("sync", f"gT1k{rng // 8}")
        dma("sync", "sFA0" if par == 0 else "sFA1", gslv, gTgr[:, :, rng * 128:(rng + 1) * 128], w=(gt,))
        B.war("sync", ut)
        B.rd("sync", f"gT2k{rng // 8}")
        dma("sync", "sFA0" if par == 0 else "sFA1", uslv, gTur[:, :, rng * 128:(rng + 1) * 128], w=(ut,))
        for n in range(4):
            pgi = (2 * n) % 4; pui = (2 * n + 1) % 4
            B.rd("tensor", gt, ut, *XYTOK)
            B.war("tensor", f"pW{pgi}", f"pW{pui}")
            for p0 in range(KC):
                st0 = (p0 == 0); sp0 = (p0 == KC - 1)
                fn1 = lambda e, p=pW[pgi], l=gslv[:, p0, :], \
                    i=xyc(p0)[:, n * 512:(n + 1) * 512], s0=st0, s1=sp0: \
                    e.matmul(p[:], l, i, start=s0, stop=s1)
                fn2 = lambda e, p=pW[pui], l=uslv[:, p0, :], \
                    i=xyc(p0)[:, n * 512:(n + 1) * 512], s0=st0, s1=sp0: \
                    e.matmul(p[:], l, i, start=s0, stop=s1)
                if sp0:
                    B.op("tensor", fn1, "st", w=(f"pW{pgi}",),
                         r=(gt,) + XYTOK)
                    B.op("tensor", fn2, "st", w=(f"pW{pui}",), r=(ut,))
                else:
                    B.plain("tensor", lambda e, f=fn1: f(e))
                    B.plain("tensor", lambda e, f=fn2: f(e))
            np_ = n % 2
            sgv = sgv_[np_]; gcv = gcv_[np_]; ttv = ttv_[np_]; itv = itv_[np_]
            et0 = EQ[0] if np_ == 0 else EV[0]
            et1 = EQ[1] if np_ == 0 else EV[1]
            B.op("scalar", lambda e, p=pW[pgi], o=sgv: e.activation(o, p[:], AF.Sigmoid),
                 "ss", w=(et0,), r=(f"pW{pgi}",))
            B.op("scalar", lambda e, p=pW[pgi], o=gcv: e.copy(out=o, in_=p[:]),
                 "ss", w=(et0,), r=(f"pW{pgi}",))
            B.op("vector", lambda e, o=ttv, a_=sgv, b_=gcv: e.tensor_mul(o, a_, b_),
                 "sv", w=(et1,), r=(et0,))
            B.op("vector", lambda e, p=pW[pui], o=itv, t=ttv: e.tensor_mul(o, t, p[:]),
                 "sv", w=(et1,), r=(f"pW{pui}", et1))
            dma("gpsimd", "sIN0" if np_ == 0 else "sIN1",
                inter_dr[rng * 128:(rng + 1) * 128,
                         n * 512:(n + 1) * 512], itv,
                r=(et1,), w=("inter",))
        if rng >= 32:
            bounce_dn_pairs((rng - 32) * 4, 4)
    if "d_inter" in dbg:
        B.rd("sync", "inter")
        for c in range(II // 128):
            dma("sync", "sq", dbg["d_inter"][c * 128:(c + 1) * 128, :],
                inter_dr[c * 128:(c + 1) * 128, :])

    if "d_gw" in dbg:
        B.rd("sync", "G1")
        for c in range(II // 128):
            dma("sync", "sq", dbg["d_gw"][c * 128:(c + 1) * 128, :],
                g_gate[c * 128:(c + 1) * 128, :])
    if "d_dw" in dbg:
        B.rd("sync", "G0")
        for c in range(KC):
            dma("sync", "sq", dbg["d_dw"][c * 128:(c + 1) * 128, :],
                g_dn[c * 128:(c + 1) * 128, :])

    gTdr = gT_dn.rearrange("(kc s) d -> s kc d", s=128)
    for half in range(2):
        B.war("sync", *(ABTOK + XYTOK + APTOK))
        B._wait("sync", "sIN0", sems["sIN0"].cnt)
        B._wait("sync", "sIN1", sems["sIN1"].cnt)
        for grp in range(8):
            src = inter_dr[grp * 1024:(grp + 1) * 1024,
                           half * 1024:(half + 1) * 1024]
            dma("sync", "sFBI", ABi[:, grp * 8:(grp + 1) * 8, :],
                src.rearrange("(ic s) t -> s ic t", s=128), w=(f"abg{grp}",))
        for do in range(KC):
            dp = do % 2
            bk = (pW[0], pW[1]) if dp == 0 else (pW[2], pW[3])
            bt = ("pW0", "pW1") if dp == 0 else ("pW2", "pW3")
            dsl = ARENA[:, 0:8192] if dp == 0 else None
            dsl = ARENA[:, 0:8192]
            dslv = dsl.rearrange("p (kc d) -> p kc d", kc=64)
            B.war("sync", *ARTOK)
            B.rd("sync", "gT0")
            dma("sync", "sFBD", dslv, gTdr[:, :, do * 128:(do + 1) * 128], w=ARTOK)
            B.rd("tensor", *ARTOK)
            B.rd("tensor", *ABTOK)
            B.war("tensor", *bt)
            for kc in range(64):
                st0 = (kc == 0); sp0 = (kc == 63)
                for n in range(2):
                    fn = (lambda e, p=bk[n], l=dslv[:, kc, :],
                          i=ABi[:, kc, n * 512:(n + 1) * 512], s0=st0, s1=sp0:
                          e.matmul(p[:], l, i, start=s0, stop=s1))
                    if sp0:
                        B.op("tensor", fn, "st", w=(bt[n],),
                             r=ARTOK + ABTOK)
                    else:
                        B.plain("tensor", lambda e, f=fn: f(e))
            xfull = XFT[:, 0:1024]
            B.war("gpsimd", "XF0")
            dma("gpsimd", "sFX", xfull,
                xmid_dr[do * 128:(do + 1) * 128, half * 1024:(half + 1) * 1024],
                w=("XF0",))
            for n in range(2):
                B.rd("vector", bt[n], "XF0")
                B.op("vector", lambda e, p=bk[n], xs=xfull[:, n * 512:(n + 1) * 512]:
                     e.tensor_add(xs, p[:], xs),
                     "sv", w=("XF0",), r=(bt[n],))
            B.rd("gpsimd", "XF0")
            dma("gpsimd", "sOT", oT[do * 128:(do + 1) * 128,
                                    half * 1024:(half + 1) * 1024],
                xfull, r=("XF0",), w=("oT",))

    assert _pump["next"] == NSUP, _pump["next"]
    for snm in ("sq", "sqG", "sqG1", "sIN0", "sIN1", "sBD", "sSG", "sOT",
                "sDBG"):
        if sems[snm].cnt:
            B.streams["sync"].append(
                lambda e, h=sems[snm].h, c=sems[snm].cnt: e.wait_ge(h, c))

    with nc.Block() as block:
        @block.sync
        def _(eng):
            for fn in B.streams["sync"]:
                fn(eng)
        @block.gpsimd
        def _(eng):
            for fn in B.streams["gpsimd"]:
                fn(eng)
        @block.vector
        def _(eng):
            for fn in B.streams["vector"]:
                fn(eng)
        @block.scalar
        def _(eng):
            for fn in B.streams["scalar"]:
                fn(eng)
        @block.tensor
        def _(eng):
            for fn in B.streams["tensor"]:
                fn(eng)
    for c in reversed(ctxs):
        c.__exit__(None, None, None)
    _NC_CACHE[key] = nc
    return nc

def kernel(L0_attn_x, L0_attn_Wq, L0_attn_Wv, L0_attn_Wg, L0_attn_Wo,
           L0_attn_beta, L0_attn_ln_w, L0_attn_ln_b,
           L0_ffn_gate_cp, L0_ffn_up_cp, L0_ffn_down_cp, _debug=()):
    tabs = _host_tables()
    x = np.ascontiguousarray(np.asarray(L0_attn_x, dtype=np.float32))
    assert x.shape == (4, 4096, 2048)
    t5g = _t5_planes(L0_ffn_gate_cp)
    t5u = _t5_planes(L0_ffn_up_cp)
    t5d = _t5_planes(L0_ffn_down_cp)
    lnw = np.ascontiguousarray(np.asarray(L0_attn_ln_w, np.float32).reshape(KC, 128).T)
    lnb = np.ascontiguousarray(np.asarray(L0_attn_ln_b, np.float32).reshape(KC, 128).T)
    shared = dict(
        wq=np.ascontiguousarray(np.asarray(L0_attn_Wq, np.float32)),
        wv=np.ascontiguousarray(np.asarray(L0_attn_Wv, np.float32)),
        wg=np.ascontiguousarray(np.asarray(L0_attn_Wg, np.float32)),
        wo=np.ascontiguousarray(np.asarray(L0_attn_Wo, np.float32)),
        beta=np.ascontiguousarray(np.asarray(L0_attn_beta, np.float32).reshape(1, H)),
        lnw=lnw, lnb=lnb, jrev=tabs["jrev"], eye=tabs["eye"],
        ones128=tabs["ones128"], onescol=tabs["onescol"],
        f1t=tabs["f1"], f2t=tabs["f2"], f3t=tabs["f3"], mskt=tabs["msk"],
        mskct=tabs["mskc"], phit=tabs["phi"], jrt=tabs["jrt"],
        t5g=t5g, t5u=t5u, t5d=t5d,
    )
    zeros_pref = np.zeros((T, D), np.float32)
    in_maps = []
    for core in range(8):
        b = core // 2; hf = core % 2
        m = dict(shared)
        m["xT"] = np.ascontiguousarray(x[b, hf * T:(hf + 1) * T, :].T)
        m["xp"] = zeros_pref if hf == 0 else np.ascontiguousarray(x[b, 0:T, :])
        in_maps.append(m)
    nc = build_nc(debug=_debug)
    res = run_bass_kernel_spmd(nc, in_maps, core_ids=list(range(8)))
    out = np.empty((4, 4096, 2048), np.float32)
    for core in range(8):
        b = core // 2; hf = core % 2
        out[b, hf * T:(hf + 1) * T, :] = res.results[core]["oT"].T
    if _debug:
        kernel._dbg = res.results
    return out
